# version 15
# baseline (speedup 1.0000x reference)
import math
import numpy as np
import ml_dtypes

import concourse.bass as bass
import concourse.mybir as mybir
from concourse.bass_utils import run_bass_kernel_spmd

F32 = mybir.dt.float32
BF16 = mybir.dt.bfloat16
AF = mybir.ActivationFunctionType
ALU = mybir.AluOpType
AX = mybir.AxisListType

NCORES = 8
S = 2048
D = 4096
HD = 128
NH = 32
NKV = 8
GRP = NH // NKV
INT = 8192
LAT = 2048
HALF = 1024
NT = 8
NCH = 32
EPS = 1e-5
BND = 611
NEG = float(np.finfo(np.float32).min)


def _panels(w):
    Kd, Md = w.shape
    a = w.reshape(Kd // 128, 128, Md // 128, 128)
    a = a.transpose(2, 1, 0, 3)
    return np.ascontiguousarray(a.reshape(Md // 128, 128, Kd).astype(np.float32))


def _host_consts():
    inv = 1.0 / 10000.0 ** (np.arange(0, HD, 2, dtype=np.float64) / HD)
    pos = np.arange(S, dtype=np.float64)
    fr = pos[:, None] * inv[None, :]
    emb = np.concatenate([fr, fr], -1)
    cos = np.cos(emb).astype(np.float32).T
    sin = np.sin(emb).astype(np.float32).T
    sc = 1.0 / math.sqrt(HD)
    kpos = np.arange(S)
    ks = np.where(kpos < 35, 0.5, np.where(kpos < BND, 1.5, 1.0)).astype(np.float32)
    c = {}
    c["cosq"] = (cos * sc).astype(ml_dtypes.bfloat16)
    c["sinq"] = (sin * sc).astype(ml_dtypes.bfloat16)
    c["cosk"] = cos.astype(ml_dtypes.bfloat16)
    c["sink"] = sin.astype(ml_dtypes.bfloat16)
    c["ksbc"] = np.ascontiguousarray(np.broadcast_to(ks, (128, S))).astype(
        ml_dtypes.bfloat16)
    inv_ks = (1.0 / ks[:640]).astype(np.float32)
    c["invks"] = np.ascontiguousarray(np.broadcast_to(inv_ks, (128, 640)))
    m4 = np.where((np.arange(512, 640)[:, None] >= BND), 1.0, inv_ks[None, :])
    c["invt4"] = np.ascontiguousarray(m4.astype(np.float32))
    tri = np.where(np.arange(128)[:, None] >= np.arange(128)[None, :], 0.0, NEG)
    c["tri"] = tri.astype(np.float32)
    R = np.zeros((128, 128), dtype=np.float32)
    for i in range(64):
        R[i, i + 64] = -1.0
        R[i + 64, i] = 1.0
    c["rotT"] = np.ascontiguousarray(R.T).astype(ml_dtypes.bfloat16)
    c["idbf"] = np.eye(128, dtype=np.float32).astype(ml_dtypes.bfloat16)
    c["idf32"] = np.eye(128, dtype=np.float32)
    c["onesrow_bf"] = np.ones((1, 128), dtype=np.float32).astype(ml_dtypes.bfloat16)
    c["onesrow_f"] = np.ones((1, 128), dtype=np.float32)
    c["onescol"] = np.ones((128, 1), dtype=np.float32)
    c["onescol_bf"] = np.ones((128, 1), dtype=np.float32).astype(ml_dtypes.bfloat16)
    trik = np.arange(128)[:, None]
    triq = np.arange(512)[None, :] % 128
    c["tri4"] = np.ascontiguousarray(
        np.where(trik <= triq, 0.0, NEG).astype(np.float32)).astype(
        ml_dtypes.bfloat16)
    ks640 = np.where(np.arange(640) < 35, 0.5,
                     np.where(np.arange(640) < BND, 1.5, 1.0)).astype(np.float32)
    f4 = np.ones((128, 5 * 116), dtype=np.float32)
    for kc in range(5):
        kvec = ks640[kc * 128:(kc + 1) * 128]
        for a in range(4):
            f4[:, kc * 116 + a * 29: kc * 116 + (a + 1) * 29] = kvec[:, None]
    c["t4fT"] = np.ascontiguousarray(f4).astype(ml_dtypes.bfloat16)
    return c


def build_core_program(dbg=False):
    nc = bass.Bass("TRN2", target_bir_lowering=False, detect_race_conditions=False)
    I = {}

    def inp(name, shape, dt=F32):
        I[name] = nc.dram_tensor(name, shape, dt, kind="ExternalInput").ap()
        return I[name]

    x_in = inp("x_in", [S, D])
    wq_p = inp("wq_p", [32, 128, D])
    wk_p = inp("wk_p", [8, 128, D])
    wv_p = inp("wv_p", [8, 128, D])
    wo_p = inp("wo_p", [32, 128, D])
    up_p = inp("up_p", [64, 128, D])
    down_p = inp("down_p", [32, 128, INT])
    fc1_p = inp("fc1_p", [16, 128, D])
    fc2_p = inp("fc2_p", [32, 128, LAT])
    ex_p = inp("ex_p", [16, 128, LAT])
    gate_p = inp("gate_p", [128, 256])
    gamma_fm = inp("gamma_fm", [128, 32])
    beta_fm = inp("beta_fm", [128, 32])
    c_cosq = inp("cosq", [128, S], BF16)
    c_sinq = inp("sinq", [128, S], BF16)
    c_cosk = inp("cosk", [128, S], BF16)
    c_sink = inp("sink", [128, S], BF16)
    c_ksbc = inp("ksbc", [128, S], BF16)
    c_invks = inp("invks", [128, 640])
    c_invt4 = inp("invt4", [128, 640])
    c_tri = inp("tri", [128, 128])
    c_rotT = inp("rotT", [128, 128], BF16)
    c_idbf = inp("idbf", [128, 128], BF16)
    c_idf32 = inp("idf32", [128, 128])
    c_onesrow = inp("onesrow_bf", [1, 128], BF16)
    c_onesrowf = inp("onesrow_f", [1, 128])
    c_onescol = inp("onescol", [128, 1])
    c_onescolb = inp("onescol_bf", [128, 1], BF16)
    c_tri4 = inp("tri4", [128, 512], BF16)
    c_t4fT = inp("t4fT", [128, 580], BF16)

    out = nc.dram_tensor("out", [S, D], F32, kind="ExternalOutput").ap()

    kind_scr = "ExternalOutput" if dbg else "Internal"
    qsp = nc.dram_tensor("qsp", [16, 128, 32, 128], BF16, kind=kind_scr).ap()
    ao_nat = nc.dram_tensor("ao_nat", [S, D], BF16, kind=kind_scr).ap()
    y_nat = nc.dram_tensor("y_nat", [S, D], F32, kind=kind_scr).ap()
    s0sp = nc.dram_tensor("s0sp", [32, 128, S], F32, kind=kind_scr).ap()
    if dbg:
        d_xnT = nc.dram_tensor("d_xnT", [128, 32768], BF16, kind="ExternalOutput").ap()
        d_ksv = nc.dram_tensor("d_ksv", [128, 32768], BF16, kind="ExternalOutput").ap()
        d_attnT = nc.dram_tensor("d_attnT", [128, 32768], BF16,
                                 kind="ExternalOutput").ap()
        d_zst = nc.dram_tensor("d_zst", [2, 128, 32768], BF16,
                               kind="ExternalOutput").ap()
        d_sst = nc.dram_tensor("d_sst", [2, 128, 32768], BF16,
                               kind="ExternalOutput").ap()
        d_xlat = nc.dram_tensor("d_xlat", [128, 16384], BF16,
                                kind="ExternalOutput").ap()
        d_moe = nc.dram_tensor("d_moe", [128, 16384], BF16, kind="ExternalOutput").ap()
        d_rw = nc.dram_tensor("d_rw", [128, 1024], F32, kind="ExternalOutput").ap()
        d_bc = nc.dram_tensor("d_bc", [4, 128, 1024], F32, kind="ExternalOutput").ap()
        d_mom = nc.dram_tensor("d_mom", [128, 64], F32, kind="ExternalOutput").ap()
        d_cc = nc.dram_tensor("d_cc", [128, 8], F32, kind="ExternalOutput").ap()
        d_tm = nc.dram_tensor("d_tm", [128, 32], F32, kind="ExternalOutput").ap()

    from contextlib import ExitStack
    with ExitStack() as ctx:
        heap = ctx.enter_context(nc.sbuf_tensor("heap", [128, 180224],
                                                mybir.dt.uint8))

        def hv(off, shape, dt):
            nb = {F32: 4, BF16: 2}[dt]
            assert off + shape[1] * nb <= 180224, f"heap view oob: {off}"
            return heap.ap()[0:shape[0], off:off + shape[1] * nb].bitcast(dt)

        H0, H1, H2, H3 = 0, 65536, 131072, 163840

        sb = lambda n, s, d=F32: ctx.enter_context(nc.sbuf_tensor(n, s, d))
        p_cosq = sb("p_cosq", [128, S], BF16)
        p_sinq = sb("p_sinq", [128, S], BF16)
        p_cosk = sb("p_cosk", [128, S], BF16)
        p_sink = sb("p_sink", [128, S], BF16)
        p_ksbc = sb("p_ksbc", [128, S], BF16)
        p_invks = sb("p_invks", [128, 640])
        p_rotT = sb("p_rotT", [128, 128], BF16)
        p_idbf = sb("p_idbf", [128, 128], BF16)
        p_idf32 = sb("p_idf32", [128, 128])
        p_ones = sb("p_ones", [1, 128], BF16)
        p_onesf = sb("p_onesf", [1, 128])
        p_onescol = sb("p_onescol", [128, 1])
        p_onescolb = sb("p_onescolb", [128, 1], BF16)
        p_tri4 = sb("p_tri4", [128, 512], BF16)
        p_t4fT = sb("p_t4fT", [128, 580], BF16)
        p_gfm = sb("p_gfm", [128, 32])
        p_bfm = sb("p_bfm", [128, 32])
        p_gatep = sb("p_gatep", [128, 256], BF16)
        statsW = sb("statsW", [128, 128])
        bW = sb("bW", [128, 64])
        constrow = sb("constrow", [1, 16])
        constcol = sb("constcol", [128, 8])
        t128a = sb("t128a", [128, 1])
        t128b = sb("t128b", [128, 1])
        t128c = sb("t128c", [128, 1])
        t128d = sb("t128d", [128, 1])
        rwtop = sb("rwtop", [128, 8])
        rw_tm = sb("rw_tm", [128, 8])
        qsqs = sb("qsqs", [128, 256], BF16)
        km_blk = sb("km_blk", [1, 128])
        km_blkp = sb("km_blkp", [1, 40])
        kmg8b = sb("kmg8b", [1, 8])
        qm1 = sb("qm1", [1, 1])
        qm2 = sb("qm2", [1, 2])
        kmg8 = sb("kmg8", [1, 8])
        u2row = sb("u2row", [1, 8])
        urow_bf = sb("urow_bf", [1, 8], BF16)
        negU8 = sb("negU8", [128, 8])
        rsum = sb("rsum", [128, 1])
        p_eps = sb("p_eps", [128, 1])

        PBt = ctx.enter_context(nc.psum_tensor("PBp", [128, 2048], F32))
        PMt = ctx.enter_context(nc.psum_tensor("PMp", [128, 1024], F32))
        PTt = ctx.enter_context(nc.psum_tensor("PTp", [128, 2048], BF16))
        PB = PBt.ap()
        PM = PMt.ap()
        PT = PTt.ap()
        PTf = PT.bitcast(F32)

        sems = {}
        cnt = {}
        lastw = {}
        CKEYS = ("pe", "vv", "ss", "gg", "fin")
        DKEYS = ("cst", "cstg", "dwa", "dwb", "dwc", "dwd", "sxa", "sqs", "qb0", "qb1", "sna",
                 "sao", "sx1", "syw", "sv1", "s0a", "s0b", "sv2", "sy0", "sy1",
                 "st0", "st1", "dbg")
        for k in CKEYS + DKEYS:
            sems[k] = ctx.enter_context(nc.semaphore(k))
            cnt[k] = 0

        pe, vv, ss, gg, dx = nc.tensor, nc.vector, nc.scalar, nc.gpsimd, nc.sync
        EALL = (pe, vv, ss, gg, dx)

        def inc(instr, key, n=None):
            if n is None:
                n = 16 if key in DKEYS else 1
            instr.then_inc(sems[key], n)
            cnt[key] += n
            return cnt[key]

        def w(eng, key, val=None):
            if val is None:
                val = cnt[key]
            if val <= 0:
                return
            ek = (id(eng), key)
            if lastw.get(ek, -1) >= val:
                return
            eng.wait_ge(sems[key], val)
            lastw[ek] = val

        def barrier(*engines):
            for e in engines or EALL:
                for k in CKEYS[:-1] + DKEYS:
                    w(e, k)

        def vsync(instr):
            c = inc(instr, "vv")
            w(vv, "vv", c)
            return c

        gst = {"ptvv": 0, "pm_vv": 0, "ptp": 0, "ptvv2": [0, 0], "ptss2": [0, 0]}

        def pt_begin():
            par = gst["ptp"]
            gst["ptp"] = 1 - par
            w(pe, "vv", gst["ptvv2"][par])
            return PT[:, par * 1024:(par + 1) * 1024], par

        def pt_done(par, copy_cnt):
            gst["ptvv2"][par] = copy_cnt
            gst["ptvv"] = copy_cnt

        for dst, src in [
            (p_cosq, c_cosq), (p_sinq, c_sinq), (p_cosk, c_cosk), (p_sink, c_sink),
            (p_ksbc, c_ksbc), (p_invks, c_invks),
            (p_rotT, c_rotT), (p_idbf, c_idbf), (p_idf32, c_idf32),
            (p_ones, c_onesrow), (p_onesf, c_onesrowf), (p_onescol, c_onescol),
            (p_onescolb, c_onescolb), (p_tri4, c_tri4), (p_t4fT, c_t4fT),
            (p_gfm, gamma_fm), (p_bfm, beta_fm),
        ]:
            inc(dx.dma_start(out=dst[:], in_=src), "cst")
        inc(gg.dma_start(out=p_gatep[:], in_=gate_p), "cstg")

        w(vv, "cst")
        if dbg:
            vv.memset(heap.ap()[:, :].bitcast(F32), 0.0)
        vv.memset(p_eps[:], EPS)
        vv.memset(statsW.ap()[:, 0:128:4], 1.0)
        vv.tensor_copy(statsW.ap()[:, 1:128:4], p_gfm[:])
        vv.tensor_mul(statsW.ap()[:, 2:128:4], p_gfm[:], p_gfm[:])
        vv.tensor_mul(statsW.ap()[:, 3:128:4], p_gfm[:], p_bfm[:])
        vv.tensor_copy(bW.ap()[:, 0:64:2], p_bfm[:])
        inc(vv.tensor_mul(bW.ap()[:, 1:64:2], p_bfm[:], p_bfm[:]), "vv")
        w(pe, "vv")
        cols = [(statsW, 4, 1), (statsW, 4, 2), (statsW, 4, 3), (bW, 2, 0), (bW, 2, 1)]
        for j, (mat, step, cofs) in enumerate(cols):
            for cc in range(32):
                mm = pe.matmul(PM[0:1, j:j + 1],
                               mat.ap()[:, cc * step + cofs: cc * step + cofs + 1],
                               p_onescol[:], start=(cc == 0), stop=(cc == 31))
        inc(mm, "pe")
        w(vv, "pe")
        vv.tensor_copy(constrow.ap()[0:1, 0:5], PM[0:1, 0:5])
        inc(vv.tensor_scalar(constrow.ap()[0:1, 8:13], constrow.ap()[0:1, 0:5],
                             1.0 / D, None, op0=ALU.mult), "vv")
        w(pe, "vv")
        mm = pe.matmul(PM[:, 0:5], p_onesf[:], constrow.ap()[0:1, 8:13],
                       start=True, stop=True)
        inc(mm, "pe")
        w(vv, "pe")
        inc(vv.tensor_copy(constcol.ap()[:, 0:5], PM[:, 0:5]), "vv")
        CGD, CG2D, CGBD, CBD, CB2D = (constcol.ap()[:, j:j + 1] for j in range(5))

        wpan = [hv(H3, [128, 4096], BF16), hv(H3 + 8192, [128, 4096], BF16)]
        wpan4 = [hv(H3 + i * 4096, [128, 2048], BF16) for i in range(4)]
        PAN = {"pe": 0}

        DWK = ("dwa", "dwb", "dwc", "dwd")

        def gemm(ms_count, cs_count, panels_ap, rhs_fn, evac_fn, wait_rhs=None,
                 pan_w=None):
            pw = pan_w if pan_w is not None else cs_count * 128
            hpw = pw // 2
            hcs = (cs_count + 1) // 2
            pe_after = {}
            psfree = {}
            tail = [None]
            dwcnt = {}

            def issue_panels(m):
                if m >= ms_count:
                    return
                for hpi in range(2):
                    hp = 2 * m + hpi
                    bi = hp % 4
                    gate_m = (hp - 4) // 2
                    if gate_m >= 0:
                        w(gg, "pe", pe_after[gate_m])
                    else:
                        w(gg, "pe", PAN["pe"])
                    lo = hpi * hpw
                    dwcnt[hp] = inc(gg.dma_start(
                        out=wpan4[bi][:, 0:pw - lo if hpi else hpw],
                        in_=panels_ap[m, :, lo:lo + (pw - lo if hpi else hpw)]),
                        DWK[bi])

            issue_panels(0)
            issue_panels(1)
            for m in range(ms_count):
                issue_panels(m + 1)
                if wait_rhs is not None:
                    wait_rhs(m)
                if m >= 2:
                    w(pe, psfree[m - 2][0], psfree[m - 2][1])
                ps = PB[:, 0:1024] if m % 2 == 0 else PB[:, 1024:2048]
                b0, b1 = (2 * m) % 4, (2 * m + 1) % 4
                for tc in range(2):
                    pv = ps[:, tc * 512:(tc + 1) * 512]
                    for cc in range(cs_count):
                        if cc == 0:
                            w(pe, DWK[b0], dwcnt[2 * m])
                        if cc == hcs:
                            w(pe, DWK[b1], dwcnt[2 * m + 1])
                        if cc < hcs:
                            lhs = wpan4[b0][:, cc * 128:(cc + 1) * 128]
                        else:
                            lhs = wpan4[b1][:, (cc - hcs) * 128:(cc - hcs + 1) * 128]
                        mm = pe.matmul(pv, lhs, rhs_fn(cc, tc),
                                       start=(cc == 0), stop=(cc == cs_count - 1))
                pe_after[m] = inc(mm, "pe")
                r = evac_fn(m, ps, pe_after[m])
                if isinstance(r[0], tuple):
                    psfree[m], newtail = r
                else:
                    psfree[m], newtail = r, None
                if tail[0] is not None:
                    tail[0]()
                tail[0] = newtail
            if tail[0] is not None:
                tail[0]()
            PAN["pe"] = pe_after[ms_count - 1]

        def ln_tm(x_ap, width, sq_discard_ap, out_ap):
            w(ss, "vv")
            vsync(vv.reduce_sum(t128a[:], x_ap, axis=AX.X))
            sq = inc(ss.activation(sq_discard_ap, x_ap, AF.Square,
                                   accum_out=t128b[:]), "ss")
            vsync(vv.tensor_scalar(t128a[:], t128a[:], 1.0 / width, None,
                                   op0=ALU.mult))
            w(vv, "ss", sq)
            vsync(vv.tensor_scalar(t128b[:], t128b[:], 1.0 / width, None,
                                   op0=ALU.mult))
            v1 = vsync(vv.scalar_tensor_tensor(t128d[:], t128a[:], t128a[:], t128b[:],
                                               op0=ALU.mult, op1=ALU.subtract))
            w(ss, "vv", v1)
            s1 = inc(ss.activation(t128c[:], t128d[:], AF.Sqrt, bias=p_eps[:], scale=-1.0),
                     "ss")
            w(vv, "ss", s1)
            vsync(vv.reciprocal(t128c[:], t128c[:]))
            v2 = vsync(vv.scalar_tensor_tensor(t128d[:], t128a[:], -1.0, t128c[:],
                                               op0=ALU.mult, op1=ALU.mult))
            return inc(vv.tensor_scalar(out_ap, x_ap, t128c[:], t128d[:],
                                        op0=ALU.mult, op1=ALU.add), "vv")

        xnT = hv(H0, [128, 32768], BF16)
        KsV = hv(H1, [128, 32768], BF16)
        Ks = KsV[:, 0:16384]
        Vst = KsV[:, 16384:32768]
        stgA2 = [hv(H2, [128, 4096], BF16), hv(H2 + 8192, [128, 4096], BF16)]
        q_sb2 = [hv(H2, [128, 1024], BF16), hv(H2 + 2048, [128, 1024], BF16)]
        rot_bf = hv(H2 + 4096, [128, 1024], BF16)
        rope_t1 = hv(H2 + 6144, [128, 1024], BF16)
        xbuf = hv(H2 + 16384, [128, 4096], F32)

        st_a = {"apply": [0, 0], "tp": [0, 0], "qsb_pe": [0, 0], "qsb_gg": [0, 0],
                "t1_vv": 0, "rot_dma": {}}

        def phase_A(h):
            barrier(dx, ss, vv, pe, gg)
            for t in range(NT):
                row0 = h * HALF + t * 128
                stgA = stgA2[t % 2]
                w(dx, "vv", st_a["apply"][0])
                ld = inc(dx.dma_start(out=xbuf[:], in_=x_in[row0:row0 + 128, :]),
                         "sxa")
                w(vv, "sxa", ld)
                w(ss, "sxa", ld)
                w(ss, "pe", st_a["tp"][t % 2])
                w(vv, "pe", st_a["tp"][t % 2])
                ap = ln_tm(xbuf[:], D, stgA[:], stgA[:])
                st_a["apply"] = [ap, st_a["apply"][0]]
                w(pe, "vv", ap)
                for cg in range(4):
                    PTv, par = pt_begin()
                    for j in range(8):
                        cc = cg * 8 + j
                        tp = pe.transpose(PTv[:, j * 128:(j + 1) * 128],
                                          stgA[:, cc * 128:(cc + 1) * 128], p_idbf[:])
                    st_a["tp"][t % 2] = inc(tp, "pe")
                    w(vv, "pe", st_a["tp"][t % 2])
                    dst = xnT.rearrange("p (c r) -> p c r", c=32, r=1024)[
                        :, cg * 8:(cg + 1) * 8, t * 128:(t + 1) * 128]
                    pt_done(par, inc(vv.tensor_copy(
                        dst, PTv.rearrange("p (c r) -> p c r", c=8, r=128)), "vv"))
            xn_ready = cnt["vv"]
            barrier(ss, gg)

            def rhs_xn(ccv, tc):
                return xnT[:, ccv * 1024 + tc * 512: ccv * 1024 + (tc + 1) * 512]

            def wait_rhs(m):
                w(pe, "vv", xn_ready)

            csq = p_cosq.ap()[:, h * HALF:(h + 1) * HALF]
            snq = p_sinq.ap()[:, h * HALF:(h + 1) * HALF]
            csk = p_cosk.ap()[:, h * HALF:(h + 1) * HALF]
            snk = p_sink.ap()[:, h * HALF:(h + 1) * HALF]
            ksl = p_ksbc.ap()[:, h * HALF:(h + 1) * HALF]

            def mk_evac_rope(cs_ap, sn_ap, store_fn, dma_guard):
                def evac(m, ps, pe_cnt):
                    q_sb = q_sb2[m % 2]
                    w(ss, "pe", max(pe_cnt, st_a["qsb_pe"][m % 2]))
                    w(ss, "vv", st_a["qsb_gg"][m % 2])
                    scopy = inc(ss.activation(q_sb[:], ps, AF.Copy), "ss")

                    def tail():
                        w(pe, "ss", scopy)
                        w(pe, "vv", gst["pm_vv"])
                        pe.matmul(PM[:, 0:512], p_rotT[:], q_sb[:, 0:512],
                                  start=True, stop=True)
                        mm = pe.matmul(PM[:, 512:1024], p_rotT[:], q_sb[:, 512:1024],
                                       start=True, stop=True)
                        rotc = inc(mm, "pe")
                        st_a["qsb_pe"][m % 2] = rotc
                        w(vv, "pe", rotc)
                        if dma_guard:
                            w(vv, "sqs", st_a["rot_dma"].get(m - 1, 0))
                        gst["pm_vv"] = inc(vv.tensor_copy(rot_bf[:], PM[:, 0:1024]),
                                           "vv")
                        w(vv, "ss", scopy)
                        st_a["qsb_gg"][m % 2] = inc(
                            vv.tensor_tensor(rope_t1[:], q_sb[:], cs_ap, op=ALU.mult),
                            "vv")
                        vv.tensor_mul(rot_bf[:], rot_bf[:], sn_ap)
                        qv = inc(vv.tensor_add(rot_bf[:], rope_t1[:], rot_bf[:]),
                                 "vv")
                        st_a["t1_vv"] = qv
                        store_fn(m, qv)

                    return ("ss", scopy), tail
                return evac

            def store_q(m, qv):
                w(dx, "vv", qv)
                dstq = qsp[h * 8:(h + 1) * 8, :, m, :].rearrange("a p j -> p a j")
                st_a["rot_dma"][m] = inc(
                    dx.dma_start(out=dstq,
                                 in_=rot_bf[:].rearrange("p (a j) -> p a j", a=8)),
                    "sqs")

            def store_k(m, qv):
                dstk = Ks[:, m * 2048 + h * HALF: m * 2048 + (h + 1) * HALF]
                inc(vv.tensor_tensor(dstk, rot_bf[:], ksl, op=ALU.mult), "vv")

            gemm(32, 32, wq_p, rhs_xn, mk_evac_rope(csq, snq, store_q, True),
                 wait_rhs=wait_rhs)
            st_a["rot_dma"] = {}
            gemm(8, 32, wk_p, rhs_xn, mk_evac_rope(csk, snk, store_k, False),
                 wait_rhs=wait_rhs)

            def evac_v(m, ps, pe_cnt):
                q_sb = q_sb2[m % 2]
                w(ss, "pe", max(pe_cnt, st_a["qsb_pe"][m % 2]))
                w(ss, "vv", st_a["qsb_gg"][m % 2])
                scopy = inc(ss.activation(q_sb[:], ps, AF.Copy), "ss")

                def tail():
                    w(pe, "ss", scopy)
                    PTv, par = pt_begin()
                    for j in range(8):
                        tp = pe.transpose(PTv[:, j * 128:(j + 1) * 128],
                                          q_sb[:, j * 128:(j + 1) * 128], p_idbf[:])
                    tpc = inc(tp, "pe")
                    st_a["qsb_pe"][m % 2] = tpc
                    w(vv, "pe", tpc)
                    dstv = Vst.rearrange("p (a r) -> p a r", a=16, r=1024)[
                        :, h * 8:(h + 1) * 8, m * 128:(m + 1) * 128]
                    pt_done(par, inc(vv.tensor_copy(
                        dstv, PTv.rearrange("p (a r) -> p a r", a=8, r=128)), "vv"))

                return ("ss", scopy), tail

            gemm(8, 32, wv_p, rhs_xn, evac_v, wait_rhs=wait_rhs)
            if dbg and h == 0:
                barrier(dx)
                inc(dx.dma_start(out=d_xnT, in_=xnT[:]), "dbg")
                inc(dx.dma_start(out=d_ksv, in_=KsV[:]), "dbg")

        attnT = hv(H0, [128, 32768], BF16)
        PTst = hv(H2, [128, 8192], BF16)
        qblk = [hv(H2 + 16384, [128, 4096], BF16), hv(H2 + 24576, [128, 4096], BF16)]
        Kpfx = hv(H3, [128, 5120], BF16)
        sqscr = hv(H3 + 10240, [128, 2048], BF16)
        rsbf = hv(H3 + 10240, [1, 512], F32)
        rrsb = hv(H3 + 14336, [128, 512], F32)

        def phase_B(h):
            barrier(dx, pe, vv, ss, gg)
            st_b = {"av_vv": [0, 0], "expfree": [0, 0, 0, 0],
                    "pvprev": {}, "sumfree": 0, "rrsb_vv": 0}
            kpfx_done = 0
            if h == 0:
                for m in range(8):
                    kpfx_done = inc(vv.tensor_tensor(
                        Kpfx[:, m * 640:(m + 1) * 640],
                        Ks[:, m * 2048: m * 2048 + 640],
                        p_invks.ap()[:, 0:640], op=ALU.mult), "vv")
            nkey = (h + 1) * 1024
            kb_done = 0
            for g in range(8):
                sqc = inc(vv.tensor_mul(sqscr[:, 0:nkey],
                                        Ks[:, g * 2048: g * 2048 + nkey],
                                        Ks[:, g * 2048: g * 2048 + nkey]), "vv")
                w(pe, "vv", cnt["vv"])
                for c4 in range(0, nkey, 512):
                    mm = pe.matmul(PB[0:1, c4:c4 + 512], p_onescolb[:],
                                   sqscr[:, c4:c4 + 512], start=True, stop=True)
                mmn = inc(mm, "pe")
                w(vv, "pe", mmn)
                kb_done = vsync(vv.reduce_max(
                    km_blk.ap()[0:1, g * 16: g * 16 + nkey // 128],
                    PB[0:1, 0:nkey].rearrange("p (a r) -> p a r", a=nkey // 128),
                    axis=AX.X))
            if h == 0:
                for g in range(8):
                    sqp = inc(vv.tensor_mul(sqscr[:, 0:640],
                                            Kpfx[:, g * 640:(g + 1) * 640],
                                            Kpfx[:, g * 640:(g + 1) * 640]),
                              "vv")
                    w(pe, "vv", cnt["vv"])
                    for c4, wdt in ((0, 512), (512, 128)):
                        mm = pe.matmul(PB[0:1, c4:c4 + wdt], p_onescolb[:],
                                       sqscr[:, c4:c4 + wdt],
                                       start=True, stop=True)
                    mmn = inc(mm, "pe")
                    w(vv, "pe", mmn)
                    kb_done = vsync(vv.reduce_max(
                        km_blkp.ap()[0:1, g * 5: g * 5 + 5],
                        PB[0:1, 0:640].rearrange("p (a r) -> p a r", a=5),
                        axis=AX.X))
            dmaq = {}
            for pre in range(2):
                dmaq[pre] = inc(dx.dma_start(
                    out=qblk[pre][:], in_=qsp[h * NT + pre]),
                    "qb0" if pre % 2 == 0 else "qb1")
            for t in range(NT):
                T = h * NT + t
                Wk = (T + 1) * 128
                nkc = T + 1
                qb = qblk[t % 2]
                w(pe, "qb0" if t % 2 == 0 else "qb1", dmaq[t])
                w(pe, "vv", kb_done)
                for par4 in range(4):
                    w(pe, "ss", st_b["expfree"][par4])
                w(vv, "qb0" if t % 2 == 0 else "qb1", dmaq[t])
                mm_of = {}
                for rnd in range(2):
                    for hh in range(16):
                        hd2 = rnd * 16 + hh
                        qs2 = qsqs.ap()[:, (hd2 % 2) * 128:(hd2 % 2) * 128 + 128]
                        w(vv, "pe", mm_of.get(hd2 - 2, 0))
                        sq = inc(vv.tensor_mul(
                            qs2, qb[:, hd2 * 128:(hd2 + 1) * 128],
                            qb[:, hd2 * 128:(hd2 + 1) * 128]), "vv")
                        w(pe, "vv", sq)
                        mm_of[hd2] = inc(
                            pe.matmul(PB[0:1, hh * 128:(hh + 1) * 128],
                                      p_onescolb[:], qs2,
                                      start=True, stop=True), "pe")
                    w(vv, "pe", cnt["pe"])
                    vsync(vv.reduce_max(qm2.ap()[0:1, rnd:rnd + 1],
                                        PB[0:1, 0:2048], axis=AX.X))
                    if rnd == 0:
                        w(pe, "vv", cnt["vv"])
                vsync(vv.reduce_max(qm1.ap()[0:1, 0:1],
                                    qm2.ap()[0:1, 0:2], axis=AX.X))
                if T <= 3:
                    for g3 in range(8):
                        rl = inc(vv.reduce_max(
                            kmg8.ap()[0:1, g3:g3 + 1],
                            km_blkp.ap()[0:1, g3 * 5: g3 * 5 + nkc],
                            axis=AX.X), "vv")
                    w(vv, "vv", rl)
                elif T == 4:
                    for g3 in range(8):
                        inc(vv.reduce_max(
                            kmg8.ap()[0:1, g3:g3 + 1],
                            km_blkp.ap()[0:1, g3 * 5: g3 * 5 + 5],
                            axis=AX.X), "vv")
                        rl = inc(vv.reduce_max(
                            kmg8b.ap()[0:1, g3:g3 + 1],
                            km_blk.ap()[0:1, g3 * 16: g3 * 16 + 5],
                            axis=AX.X), "vv")
                    w(vv, "vv", rl)
                    vsync(vv.tensor_tensor(kmg8.ap()[0:1, 0:8],
                                           kmg8.ap()[0:1, 0:8],
                                           kmg8b.ap()[0:1, 0:8], op=ALU.max))
                else:
                    for g3 in range(8):
                        rl = inc(vv.reduce_max(
                            kmg8.ap()[0:1, g3:g3 + 1],
                            km_blk.ap()[0:1, g3 * 16: g3 * 16 + nkc],
                            axis=AX.X), "vv")
                    w(vv, "vv", rl)
                vsync(vv.tensor_scalar_mul(u2row.ap()[0:1, 0:8],
                                           kmg8.ap()[0:1, 0:8],
                                           qm1.ap()[0:1, 0:1]))
                w(ss, "vv", cnt["vv"])
                sqd = inc(ss.activation(u2row.ap()[0:1, 0:8],
                                        u2row.ap()[0:1, 0:8], AF.Sqrt), "ss")
                w(vv, "ss", sqd)
                vsync(vv.tensor_scalar_mul(urow_bf.ap()[0:1, 0:8],
                                           u2row.ap()[0:1, 0:8], -1.0))
                w(pe, "vv", cnt["vv"])
                mmr = inc(pe.matmul(PB[:, 0:8], p_ones[:],
                                    urow_bf.ap()[0:1, 0:8],
                                    start=True, stop=True), "pe")
                w(vv, "pe", mmr)
                negu_done = vsync(vv.tensor_copy(negU8[:], PB[:, 0:8]))

                for g in range(8):
                    avreg = PM[:, 0:512] if g % 2 == 0 else PM[:, 512:1024]
                    rhsq = qb[:, g * 512:(g + 1) * 512]
                    expd = {}
                    pvnew = {}

                    def cons(kc, g=g, avreg=avreg, expd=expd, pvnew=pvnew):
                        w(pe, "ss", expd[kc])
                        if kc == 0:
                            w(pe, "vv", st_b["sumfree"])
                            w(pe, "vv", st_b["av_vv"][g % 2])
                        pe.matmul(PTf[0:1, 0:512], p_onescolb[:],
                                  PTst[:, kc * 512:(kc + 1) * 512],
                                  start=(kc == 0), stop=(kc == nkc - 1))
                        pvnew[kc] = inc(pe.matmul(
                            avreg,
                            Vst[:, kc * 1024 + g * 128: kc * 1024 + (g + 1) * 128],
                            PTst[:, kc * 512:(kc + 1) * 512],
                            start=(kc == 0), stop=(kc == nkc - 1)), "pe")

                    pend2 = None
                    for kc in range(nkc):
                        par = kc % 4
                        reg = PB[:, par * 512:(par + 1) * 512]
                        w(pe, "ss", st_b["expfree"][par])
                        w(pe, "vv", negu_done)
                        if T <= 4:
                            w(pe, "vv", kpfx_done)
                            lhsk = Kpfx[:, g * 640 + kc * 128:
                                        g * 640 + (kc + 1) * 128]
                        else:
                            lhsk = Ks[:, g * 2048 + kc * 128:
                                      g * 2048 + (kc + 1) * 128]
                        mmS = inc(pe.matmul(reg, lhsk, rhsq,
                                            start=True, stop=True), "pe")
                        vv_pre = 0
                        if T == 4:
                            w(vv, "pe", mmS)
                            r3 = reg.rearrange("p (a r) -> p a r", a=4,
                                               r=128)[:, :, 99:128]
                            vv_pre = inc(vv.tensor_tensor(
                                r3, r3,
                                p_t4fT.ap()[:, kc * 116:(kc + 1) * 116].rearrange(
                                    "p (a r) -> p a r", a=4, r=29),
                                op=ALU.mult), "vv")
                        if kc == nkc - 1:
                            w(vv, "pe", mmS)
                            vv_pre = inc(vv.tensor_add(reg, reg, p_tri4[:]),
                                         "vv")
                        w(ss, "pe", mmS)
                        if vv_pre:
                            w(ss, "vv", vv_pre)
                        w(ss, "pe", st_b["pvprev"].get(kc, 0))
                        expd[kc] = inc(ss.activation(
                            PTst[:, kc * 512:(kc + 1) * 512], reg, AF.Exp,
                            bias=negU8.ap()[:, g:g + 1]), "ss")
                        st_b["expfree"][par] = expd[kc]
                        if pend2 is not None:
                            cons(pend2)
                        pend2 = kc
                    cons(pend2)
                    st_b["pvprev"] = pvnew
                    pvd = cnt["pe"]
                    w(vv, "pe", pvd)
                    rc = vsync(vv.reciprocal(rsbf[0:1, 0:512],
                                             PTf[0:1, 0:512]))
                    st_b["sumfree"] = rc
                    w(pe, "vv", rc)
                    w(pe, "vv", st_b["rrsb_vv"])
                    mmr2 = inc(pe.matmul(PTf[:, 512:1024], p_onesf[:],
                                         rsbf[0:1, 0:512],
                                         start=True, stop=True), "pe")
                    w(vv, "pe", mmr2)
                    rrv = vsync(vv.tensor_copy(rrsb[:, 0:512], PTf[:, 512:1024]))
                    st_b["rrsb_vv"] = rrv
                    dsta = attnT.rearrange("p (a r) -> p a r", a=32, r=1024)[
                        :, 4 * g:4 * (g + 1), t * 128:(t + 1) * 128]
                    st_b["av_vv"][g % 2] = inc(vv.tensor_tensor(
                        dsta, avreg.rearrange("p (a r) -> p a r", a=4, r=128),
                        rrsb[:, 0:512].rearrange("p (a r) -> p a r", a=4, r=128),
                        op=ALU.mult), "vv")
                tile_pe = cnt["pe"]
                if t + 2 < NT:
                    w(dx, "pe", tile_pe)
                    dmaq[t + 2] = inc(dx.dma_start(
                        out=qblk[t % 2][:], in_=qsp[h * NT + t + 2]),
                        "qb0" if t % 2 == 0 else "qb1")
            attn_done = cnt["vv"]
            gst["ptvv2"] = [st_b["sumfree"], st_b["rrsb_vv"]]

            barrier(gg, dx)
            ao_f32 = hv(H2, [128, 1024], F32)
            ao_bf2 = [hv(H2 + 4096, [128, 1024], BF16),
                      hv(H2 + 6144, [128, 1024], BF16)]
            natstg = hv(H2 + 8192, [128, 1024], BF16)
            st_wo = {"cast_gg": 0, "tp": 0, "nat_vv": 0, "nat_dx": 0}

            def rhs_attn(ccv, tc):
                return attnT[:, ccv * 1024 + tc * 512: ccv * 1024 + (tc + 1) * 512]

            def wait_rhs_wo(m):
                w(pe, "vv", attn_done)

            def evac_wo(m, ps, pe_cnt):
                w(vv, "pe", pe_cnt)
                w(vv, "ss", st_wo["cast_gg"])
                aoc = inc(vv.tensor_copy(ao_f32[:], ps), "vv")
                w(ss, "vv", aoc)
                w(ss, "pe", st_wo["tp"])
                ao_bf = ao_bf2[m % 2]
                cast = inc(ss.activation(ao_bf[:], ao_f32[:], AF.Copy), "ss")
                st_wo["cast_gg"] = cast

                def tail():
                    w(pe, "ss", cast)
                    w(pe, "vv", st_wo["nat_vv"])
                    PTv, par = pt_begin()
                    for j in range(8):
                        tp = pe.transpose(PTv[:, j * 128:(j + 1) * 128],
                                          ao_bf[:, j * 128:(j + 1) * 128], p_idbf[:])
                    st_wo["tp"] = inc(tp, "pe")
                    w(vv, "pe", st_wo["tp"])
                    w(vv, "sna", st_wo["nat_dx"])
                    st_wo["nat_vv"] = inc(vv.tensor_copy(natstg[:], PTv), "vv")
                    pt_done(par, st_wo["nat_vv"])
                    w(dx, "vv", st_wo["nat_vv"])
                    dstn = ao_nat[h * HALF:(h + 1) * HALF,
                                  m * 128:(m + 1) * 128].rearrange(
                        "(a p) j -> p a j", p=128)
                    st_wo["nat_dx"] = inc(
                        dx.dma_start(out=dstn,
                                     in_=natstg[:].rearrange("p (a j) -> p a j",
                                                             a=8)), "sna")

                return ("vv", aoc), tail

            gemm(32, 32, wo_p, rhs_attn, evac_wo, wait_rhs=wait_rhs_wo)
            if dbg and h == 0:
                barrier(dx)
                inc(dx.dma_start(out=d_attnT, in_=attnT[:]), "dbg")

        def phase_C(h):
            barrier(dx, vv, ss, pe, gg)
            zst = hv(H1, [128, 32768], BF16)
            aot = hv(H0, [128, 4096], BF16)
            xt = hv(H0 + 8192, [128, 4096], F32)
            ant = hv(H0 + 24576, [128, 4096], F32)
            zn = hv(H0 + 40960, [128, 4096], BF16)
            sqd = hv(H0 + 49152, [128, 4096], BF16)
            st_c = {"aol_free": 0, "tp": 0, "ystore": 0, "zapply": 0, "sq_last": 0}
            for t in range(NT):
                row0 = h * HALF + t * 128
                w(dx, "vv", st_c["zapply"])
                aol = inc(dx.dma_start(out=aot[:], in_=ao_nat[row0:row0 + 128, :]),
                          "sao")
                w(dx, "vv", st_c["aol_free"])
                xl = inc(dx.dma_start(out=xt[:], in_=x_in[row0:row0 + 128, :]), "sx1")
                w(vv, "sao", aol)
                w(ss, "sao", aol)
                w(ss, "pe", st_c["tp"])
                w(ss, "vv")
                vsync(vv.reduce_sum(t128a[:], aot[:], axis=AX.X))
                sq = inc(ss.activation(sqd[:], aot[:], AF.Square, accum_out=t128b[:]),
                         "ss")
                vsync(vv.tensor_scalar(t128a[:], t128a[:], 1.0 / D, None,
                                       op0=ALU.mult))
                w(vv, "ss", sq)
                vsync(vv.tensor_scalar(t128b[:], t128b[:], 1.0 / D, None,
                                       op0=ALU.mult))
                v1 = vsync(vv.scalar_tensor_tensor(t128d[:], t128a[:], t128a[:],
                                                   t128b[:], op0=ALU.mult,
                                                   op1=ALU.subtract))
                w(ss, "vv", v1)
                s1 = inc(ss.activation(t128c[:], t128d[:], AF.Sqrt, bias=p_eps[:],
                                       scale=-1.0), "ss")
                w(vv, "ss", s1)
                vsync(vv.reciprocal(t128c[:], t128c[:]))
                v2 = vsync(vv.scalar_tensor_tensor(t128d[:], t128a[:], -1.0, t128c[:],
                                                   op0=ALU.mult, op1=ALU.mult))
                ap2 = inc(vv.tensor_scalar(ant[:], aot[:], t128c[:], t128d[:],
                                           op0=ALU.mult, op1=ALU.add), "vv")
                w(vv, "sx1", xl)
                yv = inc(vv.tensor_add(ant[:], ant[:], xt[:]), "vv")
                st_c["aol_free"] = yv
                w(dx, "vv", yv)
                st_c["ystore"] = inc(dx.dma_start(out=y_nat[row0:row0 + 128, :],
                                                  in_=ant[:]), "syw")
                st_c["zapply"] = ln_tm(ant[:], D, sqd[:], zn[:])
                w(pe, "vv", st_c["zapply"])
                for cg in range(4):
                    PTv, par = pt_begin()
                    for j in range(8):
                        cc = cg * 8 + j
                        tp = pe.transpose(PTv[:, j * 128:(j + 1) * 128],
                                          zn[:, cc * 128:(cc + 1) * 128], p_idbf[:])
                    st_c["tp"] = inc(tp, "pe")
                    w(vv, "pe", st_c["tp"])
                    dstz = zst.rearrange("p (c r) -> p c r", c=32, r=1024)[
                        :, cg * 8:(cg + 1) * 8, t * 128:(t + 1) * 128]
                    pt_done(par, inc(vv.tensor_copy(
                        dstz, PTv.rearrange("p (c r) -> p c r", c=8, r=128)), "vv"))
                w(vv, "syw", st_c["ystore"])
            if dbg:
                w(dx, "vv")
                inc(dx.dma_start(out=d_zst[h], in_=zst[:]), "dbg")
            z_ready = cnt["vv"]

            def rhs_z(ccv, tc):
                return zst[:, ccv * 1024 + tc * 512: ccv * 1024 + (tc + 1) * 512]

            def wait_z(m):
                w(pe, "vv", z_ready)

            barrier(pe)
            w(pe, "cstg", 16)
            for t in range(NT):
                for cc in range(NCH):
                    mm = pe.matmul(PM[:, t * 8:(t + 1) * 8],
                                   zst[:, cc * 1024 + t * 128: cc * 1024 + (t + 1) * 128],
                                   p_gatep[:, cc * 8:(cc + 1) * 8],
                                   start=(cc == 0), stop=(cc == NCH - 1))
            gmm = inc(mm, "pe")
            w(vv, "pe", gmm)
            for t in range(NT):
                vsync(vv.max(out=rwtop[:], in_=PM[:, t * 8:(t + 1) * 8]))
                vsync(vv.tensor_add(rw_tm.ap()[:, t:t + 1], rwtop.ap()[:, 0:1],
                                    rwtop.ap()[:, 1:2]))
            w(pe, "vv")
            for t in range(NT):
                tp = pe.transpose(PB[0:1, t * 128:(t + 1) * 128],
                                  rw_tm.ap()[:, t:t + 1], p_idf32[:])
            inc(tp, "pe")
            w(vv, "pe")
            rw_row = hv(H2, [1, 1024], BF16)
            inc(vv.tensor_copy(rw_row[0:1, :], PB[0:1, 0:1024]), "vv")
            w(pe, "vv")
            pe.matmul(PM[:, 0:512], p_ones[:], rw_row[0:1, 0:512], start=True,
                      stop=True)
            mm = pe.matmul(PM[:, 512:1024], p_ones[:], rw_row[0:1, 512:1024],
                           start=True, stop=True)
            inc(mm, "pe")
            rw_bc = hv(H2 + 2048, [128, 1024], F32)
            w(vv, "pe")
            inc(vv.tensor_copy(rw_bc[:], PM[:, 0:1024]), "vv")
            if dbg and h == 0:
                w(dx, "vv")
                inc(dx.dma_start(out=d_rw, in_=rw_bc[:]), "dbg")

            xlat = hv(H0, [128, 16384], BF16)
            barrier(pe)

            def evac_fc1(m, ps, pe_cnt):
                w(vv, "pe", pe_cnt)
                c1 = inc(vv.scalar_tensor_tensor(xlat[:, m * 1024:(m + 1) * 1024],
                                                 ps, 1.0, rw_bc[:],
                                                 op0=ALU.mult, op1=ALU.mult), "vv")
                return ("vv", c1)

            gemm(16, 32, fc1_p, rhs_z, evac_fc1, wait_rhs=wait_z)
            xlat_ready = cnt["vv"]

            barrier(pe)
            moeT = hv(H2, [128, 16384], BF16)

            def rhs_xlat(ccv, tc):
                return xlat[:, ccv * 1024 + tc * 512: ccv * 1024 + (tc + 1) * 512]

            def wait_xlat(m):
                w(pe, "vv", xlat_ready)

            def evac_ex(m, ps, pe_cnt):
                w(vv, "pe", pe_cnt)
                c1 = inc(vv.tensor_copy(moeT[:, m * 1024:(m + 1) * 1024], ps), "vv")
                return ("vv", c1)

            gemm(16, 16, ex_p, rhs_xlat, evac_ex, wait_rhs=wait_xlat)
            moe_ready = cnt["vv"]
            if dbg and h == 0:
                w(dx, "vv", moe_ready)
                inc(dx.dma_start(out=d_xlat, in_=xlat[:]), "dbg")
                inc(dx.dma_start(out=d_moe, in_=moeT[:]), "dbg")

            barrier(pe)
            sevac = hv(H0 + 32768, [128, 1024], F32)
            st_f = {"sev_dx": 0}

            def rhs_moe(ccv, tc):
                return moeT[:, ccv * 1024 + tc * 512: ccv * 1024 + (tc + 1) * 512]

            def wait_moe(m):
                w(pe, "vv", moe_ready)

            def evac_fc2(m, ps, pe_cnt):
                w(vv, "pe", pe_cnt)
                w(vv, "sv1", st_f["sev_dx"])
                sv = inc(vv.tensor_copy(sevac[:], ps), "vv")
                w(dx, "vv", sv)
                st_f["sev_dx"] = inc(dx.dma_start(
                    out=s0sp[m, :, h * HALF:(h + 1) * HALF], in_=sevac[:]), "sv1")
                return ("vv", sv)

            gemm(32, 16, fc2_p, rhs_moe, evac_fc2, wait_rhs=wait_moe, pan_w=2048)
            fc2_stores = cnt["sv1"]

            barrier(pe, vv, ss)
            w(vv, "sv1", fc2_stores)
            hst = hv(H0, [128, 32768], BF16)
            relu_s = hv(H2, [128, 1024], BF16)
            st_up = {"relu_vv": 0}

            def evac_up(m, ps, pe_cnt):
                w(ss, "pe", pe_cnt)
                w(ss, "vv", st_up["relu_vv"])
                sd = inc(ss.activation(relu_s[:], ps, AF.Relu), "ss")
                w(vv, "ss", sd)
                st_up["relu_vv"] = inc(vv.tensor_mul(hst[:, m * 1024:(m + 1) * 1024],
                                                     relu_s[:], relu_s[:]), "vv")
                return ("ss", sd)

            gemm(32, 32, up_p, rhs_z, evac_up, wait_rhs=wait_z)
            hA_ready = cnt["vv"]

            def rhs_h(ccv, tc):
                return hst[:, ccv * 1024 + tc * 512: ccv * 1024 + (tc + 1) * 512]

            barrier(pe, vv, dx)
            s0buf = [hv(H2 + 4096, [128, 1024], F32), hv(H2 + 8192, [128, 1024], F32)]
            sevac2 = hv(H2 + 12288, [128, 1024], F32)
            w(dx, "sv1", fc2_stores)
            dma_s0 = {}
            for m0 in range(2):
                dma_s0[m0] = inc(dx.dma_start(
                    out=s0buf[m0][:], in_=s0sp[m0, :, h * HALF:(h + 1) * HALF]),
                    "s0a" if m0 % 2 == 0 else "s0b")
            st_d = {"sev_dx": 0}
            pe_after = {}
            psfree = {}
            dwc6 = {}

            def issue_c6(m):
                if m >= 32:
                    return
                w(gg, "pe", pe_after[m - 2] if m >= 2 else PAN["pe"])
                dk6 = "dwa" if m % 2 == 0 else "dwb"
                dwc6[m] = inc(gg.dma_start(out=wpan[m % 2][:],
                                           in_=down_p[m, :, 0:4096]), dk6)

            issue_c6(0)
            issue_c6(1)
            for m in range(32):
                issue_c6(m + 1)
                dk = "dwa" if m % 2 == 0 else "dwb"
                w(pe, dk, dwc6[m])
                w(pe, "vv", hA_ready)
                if m >= 2:
                    w(pe, "vv", psfree[m - 2])
                ps = PB[:, 0:1024] if m % 2 == 0 else PB[:, 1024:2048]
                for tc in range(2):
                    pv = ps[:, tc * 512:(tc + 1) * 512]
                    for cc in range(32):
                        mm = pe.matmul(pv, wpan[m % 2][:, cc * 128:(cc + 1) * 128],
                                       rhs_h(cc, tc), start=(cc == 0), stop=(cc == 31))
                pe_after[m] = inc(mm, "pe")
                w(vv, "pe", pe_after[m])
                w(vv, "s0a" if m % 2 == 0 else "s0b", dma_s0[m])
                w(vv, "sv2", st_d["sev_dx"])
                sv = inc(vv.tensor_add(sevac2[:], ps, s0buf[m % 2][:]), "vv")
                psfree[m] = sv
                if m + 2 < 32:
                    w(dx, "vv", sv)
                    dma_s0[m + 2] = inc(dx.dma_start(
                        out=s0buf[m % 2][:],
                        in_=s0sp[m + 2, :, h * HALF:(h + 1) * HALF]),
                        "s0a" if m % 2 == 0 else "s0b")
                w(dx, "vv", sv)
                st_d["sev_dx"] = inc(dx.dma_start(
                    out=s0sp[m, :, h * HALF:(h + 1) * HALF], in_=sevac2[:]), "sv2")
            downA_done = pe_after[31]
            downA_stores = cnt["sv2"]

            barrier(pe, ss, vv)

            def wait_z_b(m):
                w(pe, "vv", z_ready)
                w(pe, "pe", downA_done)

            gemm(32, 32, up_p[32:64], rhs_z, evac_up, wait_rhs=wait_z_b)
            hB_ready = cnt["vv"]

            barrier(pe, vv, ss, gg, dx)
            w(dx, "sv2", downA_stores)
            sst = hv(H1, [128, 32768], BF16)
            sqst2 = [hv(H2 + 16384, [128, 1024], F32), hv(H2 + 28672, [128, 1024], F32)]
            s_f32b = [hv(H2 + 20480, [128, 1024], F32), hv(H2 + 24576, [128, 1024], F32)]
            dma_s0 = {}
            for m0 in range(2):
                dma_s0[m0] = inc(dx.dma_start(
                    out=s0buf[m0][:], in_=s0sp[m0, :, h * HALF:(h + 1) * HALF]),
                    "s0a" if m0 % 2 == 0 else "s0b")
            st_e = {"sq_ss": [0, 0], "mom_pe": [0, 0], "cast_gg": [0, 0]}
            pe_after2 = {}
            mom_tail = [None]
            dwc8 = {}

            def issue_c8(m):
                if m >= 32:
                    return
                w(gg, "pe", pe_after2[m - 2] if m >= 2 else PAN["pe"])
                dk8 = "dwa" if m % 2 == 0 else "dwb"
                dwc8[m] = inc(gg.dma_start(out=wpan[m % 2][:],
                                           in_=down_p[m, :, 4096:8192]), dk8)

            issue_c8(0)
            issue_c8(1)
            for m in range(32):
                issue_c8(m + 1)
                dk = "dwa" if m % 2 == 0 else "dwb"
                w(pe, dk, dwc8[m])
                w(pe, "vv", hB_ready)
                if m >= 2:
                    w(pe, "vv", psfree[m - 2])
                ps = PB[:, 0:1024] if m % 2 == 0 else PB[:, 1024:2048]
                for tc in range(2):
                    pv = ps[:, tc * 512:(tc + 1) * 512]
                    for cc in range(32):
                        mm = pe.matmul(pv, wpan[m % 2][:, cc * 128:(cc + 1) * 128],
                                       rhs_h(cc, tc), start=(cc == 0), stop=(cc == 31))
                pe_after2[m] = inc(mm, "pe")
                if mom_tail[0] is not None:
                    mom_tail[0]()
                    mom_tail[0] = None
                s_f32 = s_f32b[m % 2]
                sqst = sqst2[m % 2]
                w(vv, "pe", pe_after2[m])
                w(vv, "s0a" if m % 2 == 0 else "s0b", dma_s0[m])
                w(vv, "pe", st_e["mom_pe"][m % 2])
                w(vv, "ss", st_e["cast_gg"][m % 2])
                w(vv, "ss", st_e["sq_ss"][m % 2])
                sv = inc(vv.tensor_add(s_f32[:], ps, s0buf[m % 2][:]), "vv")
                psfree[m] = sv
                if m + 2 < 32:
                    w(dx, "vv", sv)
                    dma_s0[m + 2] = inc(dx.dma_start(
                        out=s0buf[m % 2][:],
                        in_=s0sp[m + 2, :, h * HALF:(h + 1) * HALF]),
                        "s0a" if m % 2 == 0 else "s0b")
                w(ss, "vv", sv)
                w(ss, "pe", st_e["mom_pe"][m % 2])
                st_e["sq_ss"][m % 2] = inc(
                    ss.activation(sqst[:], s_f32[:], AF.Square), "ss")
                sqv = st_e["sq_ss"][m % 2]
                st_e["cast_gg"][m % 2] = inc(
                    ss.activation(sst[:, m * 1024:(m + 1) * 1024], s_f32[:],
                                  AF.Copy), "ss")

                def mk_mom(mv, s_f32v, sqstv, svv, sqv2):
                    def tailf():
                        w(pe, "vv", svv)
                        w(pe, "ss", sqv2)
                        for tc in range(2):
                            pe.matmul(PM[0:4, tc * 512:(tc + 1) * 512],
                                      statsW.ap()[:, mv * 4:(mv + 1) * 4],
                                      s_f32v[:, tc * 512:(tc + 1) * 512],
                                      start=(mv == 0), stop=(mv == 31))
                            mm2 = pe.matmul(PTf[0:4, tc * 512:(tc + 1) * 512],
                                            statsW.ap()[:, mv * 4:(mv + 1) * 4],
                                            sqstv[:, tc * 512:(tc + 1) * 512],
                                            start=(mv == 0), stop=(mv == 31))
                        st_e["mom_pe"][mv % 2] = inc(mm2, "pe")
                    return tailf

                mom_tail[0] = mk_mom(m, s_f32, sqst, sv, sqv)
            mom_tail[0]()
            s_ready_gg = cnt["ss"]
            mom_done = cnt["pe"]
            if dbg:
                w(dx, "ss", s_ready_gg)
                inc(dx.dma_start(out=d_sst[h], in_=sst[:]), "dbg")

            barrier(vv, pe, ss)
            m4sb = hv(H2 + 24576, [4, 1024], F32)
            m2sb = hv(H2 + 28672, [4, 1024], F32)
            momtm = hv(H2 + 12288, [128, 64], F32)
            w(vv, "pe", mom_done)
            vv.tensor_copy(m4sb[0:4, :], PM[0:4, 0:1024])
            mc = inc(vv.tensor_copy(m2sb[0:4, :], PTf[0:4, 0:1024]), "vv")
            w(pe, "vv", mc)
            for blk in range(8):
                pe.transpose(PB[:, blk * 4:(blk + 1) * 4],
                             m4sb[0:4, blk * 128:(blk + 1) * 128], p_idf32[0:4, 0:4])
                tp = pe.transpose(PB[:, 32 + blk * 4: 32 + (blk + 1) * 4],
                                  m2sb[0:4, blk * 128:(blk + 1) * 128],
                                  p_idf32[0:4, 0:4])
            inc(tp, "pe")
            w(vv, "pe")
            inc(vv.tensor_copy(momtm[:], PB[:, 0:64]), "vv")
            Sv1 = momtm[:, 0:32:4]
            Gv1 = momtm[:, 1:32:4]
            Gv2 = momtm[:, 2:32:4]
            GvB = momtm[:, 3:32:4]
            Sv2 = momtm[:, 32:64:4]
            Hv2 = momtm[:, 34:64:4]
            tmb = H2 + 13312
            tm = lambda i: hv(tmb + i * 64, [128, 8], F32)
            mu4, rs4, mu5, rs5, tA, tB, tC, Ar, Cr, Dr = (tm(i) for i in range(10))
            invD = 1.0 / D
            vsync(vv.tensor_scalar(mu4[:], Sv1, invD, None, op0=ALU.mult))
            vsync(vv.tensor_scalar(tA[:], Sv2, invD, None, op0=ALU.mult))
            vsync(vv.tensor_mul(tB[:], mu4[:], mu4[:]))
            v1 = vsync(vv.tensor_sub(tB[:], tB[:], tA[:]))
            w(ss, "vv", v1)
            s1 = inc(ss.activation(tB[:], tB[:], AF.Sqrt, bias=p_eps[:], scale=-1.0), "ss")
            w(vv, "ss", s1)
            vsync(vv.reciprocal(rs4[:], tB[:]))
            vsync(vv.tensor_scalar(tA[:], Gv1, invD, None, op0=ALU.mult))
            vsync(vv.tensor_scalar(tB[:], mu4[:], CGD, None, op0=ALU.mult))
            vsync(vv.tensor_sub(tA[:], tA[:], tB[:]))
            vsync(vv.tensor_mul(tA[:], tA[:], rs4[:]))
            vsync(vv.tensor_scalar(mu5[:], tA[:], CBD, None, op0=ALU.add))
            vsync(vv.tensor_scalar(tA[:], Gv2, 2.0 * invD, None, op0=ALU.mult))
            vsync(vv.tensor_mul(tA[:], tA[:], mu4[:]))
            vsync(vv.tensor_scalar(tB[:], Hv2, invD, None, op0=ALU.mult))
            vsync(vv.tensor_sub(tB[:], tB[:], tA[:]))
            vsync(vv.tensor_mul(tA[:], mu4[:], mu4[:]))
            vsync(vv.tensor_scalar(tA[:], tA[:], CG2D, None, op0=ALU.mult))
            vsync(vv.tensor_add(tB[:], tB[:], tA[:]))
            vsync(vv.tensor_mul(tB[:], tB[:], rs4[:]))
            vsync(vv.tensor_mul(tB[:], tB[:], rs4[:]))
            vsync(vv.tensor_scalar(tA[:], GvB, invD, None, op0=ALU.mult))
            vsync(vv.tensor_scalar(tC[:], mu4[:], CGBD, None, op0=ALU.mult))
            vsync(vv.tensor_sub(tA[:], tA[:], tC[:]))
            vsync(vv.tensor_mul(tA[:], tA[:], rs4[:]))
            vsync(vv.tensor_scalar(tA[:], tA[:], 2.0, None, op0=ALU.mult))
            vsync(vv.tensor_add(tB[:], tB[:], tA[:]))
            vsync(vv.tensor_scalar(tB[:], tB[:], CB2D, None, op0=ALU.add))
            vsync(vv.tensor_mul(tC[:], mu5[:], mu5[:]))
            v2 = vsync(vv.tensor_sub(tC[:], tC[:], tB[:]))
            w(ss, "vv", v2)
            s2 = inc(ss.activation(tC[:], tC[:], AF.Sqrt, bias=p_eps[:], scale=-1.0), "ss")
            w(vv, "ss", s2)
            vsync(vv.reciprocal(rs5[:], tC[:]))
            vsync(vv.tensor_mul(Ar[:], rs4[:], rs5[:]))
            vsync(vv.tensor_mul(Cr[:], mu4[:], Ar[:]))
            v3 = vsync(vv.tensor_mul(Dr[:], mu5[:], rs5[:]))
            rowsbf = hv(H2, [1, 4096], BF16)
            bcA = hv(H0, [128, 1024], F32)
            bcC = hv(H0 + 4096, [128, 1024], F32)
            bcB = hv(H0 + 8192, [128, 1024], F32)
            bcD = hv(H0 + 12288, [128, 1024], F32)
            w(pe, "vv", v3)
            rowcp = 0
            for rnd, (ta_, tb_) in enumerate([(Ar, Cr), (rs5, Dr)]):
                w(pe, "vv", rowcp)
                for blk in range(8):
                    pe.transpose(PB[0:1, blk * 128:(blk + 1) * 128],
                                 ta_[:, blk:blk + 1], p_idf32[:])
                    tp = pe.transpose(PB[0:1, 1024 + blk * 128: 1024 + (blk + 1) * 128],
                                      tb_[:, blk:blk + 1], p_idf32[:])
                inc(tp, "pe")
                w(vv, "pe")
                rowcp = inc(vv.tensor_copy(rowsbf[0:1, rnd * 2048:(rnd + 1) * 2048],
                                           PB[0:1, 0:2048]), "vv")
            w(pe, "vv")
            repcp = 0
            for i, bc in enumerate([bcA, bcC, bcB, bcD]):
                w(pe, "vv", repcp)
                pe.matmul(PM[:, 0:512], p_ones[:],
                          rowsbf[0:1, i * 1024: i * 1024 + 512], start=True, stop=True)
                mm = pe.matmul(PM[:, 512:1024], p_ones[:],
                               rowsbf[0:1, i * 1024 + 512:(i + 1) * 1024],
                               start=True, stop=True)
                inc(mm, "pe")
                w(vv, "pe")
                repcp = inc(vv.tensor_copy(bc[:], PM[:, 0:1024]), "vv")
            bc_ready = cnt["vv"]
            if dbg and h == 0:
                w(dx, "vv")
                for bi, bc in enumerate([bcA, bcC, bcB, bcD]):
                    inc(dx.dma_start(out=d_bc[bi], in_=bc[:]), "dbg")
                inc(dx.dma_start(out=d_mom, in_=momtm[:]), "dbg")
                inc(dx.dma_start(out=d_cc[:, 0:5], in_=constcol.ap()[:, 0:5]), "dbg")
                inc(dx.dma_start(out=d_tm[:, 0:8], in_=mu4[:]), "dbg")
                inc(dx.dma_start(out=d_tm[:, 8:16], in_=rs4[:]), "dbg")
                inc(dx.dma_start(out=d_tm[:, 16:24], in_=mu5[:]), "dbg")
                inc(dx.dma_start(out=d_tm[:, 24:32], in_=rs5[:]), "dbg")

            barrier(vv, ss, gg, pe, dx)
            natp = hv(H0 + 16384, [128, 8192], F32)
            ofm2 = [hv(H2 + 16384, [128, 1024], F32), hv(H2 + 20480, [128, 1024], F32)]
            s32b = [hv(H2 + 24576, [128, 1024], F32), hv(H2 + 28672, [128, 1024], F32)]
            t1b = [hv(H0 + 49152, [128, 1024], F32), hv(H0 + 53248, [128, 1024], F32)]
            t3b = [hv(H0 + 57344, [128, 1024], F32), hv(H0 + 61440, [128, 1024], F32)]
            ybl = [hv(H2 + 4096, [128, 1024], F32), hv(H2 + 8192, [128, 1024], F32)]
            obl = [hv(H2, [128, 1024], F32), hv(H3, [128, 1024], F32)]
            st10 = {"t1_vv": [0, 0], "s32_vv": [0, 0], "ofm_pe": [0, 0],
                    "natp_gg": 0, "tpc": [0, 0], "ost": {}, "gga": {}}
            for mg in range(4):
                w(vv, "gg", st10["natp_gg"])
                for mi in range(8):
                    m = mg * 8 + mi
                    par = mi % 2
                    s32 = s32b[par]
                    t1 = t1b[par]
                    t3 = t3b[par]
                    ofm = ofm2[par]
                    gsl = p_gfm.ap()[:, m:m + 1]
                    bsl = p_bfm.ap()[:, m:m + 1]
                    w(ss, "vv", st10["s32_vv"][par])
                    sc = inc(ss.activation(s32[:], sst[:, m * 1024:(m + 1) * 1024],
                                           AF.Copy), "ss")
                    w(vv, "ss", sc)
                    tv = inc(vv.scalar_tensor_tensor(t1[:], s32[:], 1.0, bcA[:],
                                                     op0=ALU.mult, op1=ALU.mult),
                             "vv")
                    st10["s32_vv"][par] = tv
                    w(gg, "vv", tv)
                    w(gg, "vv", st10["t1_vv"][par])
                    g2 = inc(gg.tensor_sub(t1[:], t1[:], bcC[:]), "gg")
                    vv.scalar_tensor_tensor(t3[:], bcB[:], bsl, bcD[:],
                                            op0=ALU.mult, op1=ALU.subtract)
                    w(vv, "gg", g2)
                    w(vv, "pe", st10["ofm_pe"][par])
                    ov = inc(vv.scalar_tensor_tensor(ofm[:], t1[:], gsl, t3[:],
                                                     op0=ALU.mult, op1=ALU.add), "vv")
                    st10["t1_vv"][par] = ov
                    w(pe, "vv", ov)
                    w(pe, "vv", st10["tpc"][par])
                    pbh = PB[:, 0:1024] if par == 0 else PB[:, 1024:2048]
                    for j in range(8):
                        tp = pe.transpose(pbh[:, j * 128:(j + 1) * 128],
                                          ofm[:, j * 128:(j + 1) * 128], p_idf32[:])
                    st10["ofm_pe"][par] = inc(tp, "pe")
                    w(vv, "pe", st10["ofm_pe"][par])
                    dstn = natp.rearrange("p (a r) -> p a r", a=8, r=1024)[
                        :, :, mi * 128:(mi + 1) * 128]
                    st10["tpc"][par] = inc(
                        vv.tensor_copy(dstn, pbh.rearrange("p (a r) -> p a r",
                                                           a=8, r=128)), "vv")
                natp_done = cnt["vv"]
                for t in range(NT):
                    row0 = h * HALF + t * 128
                    c0 = mg * 1024
                    yb = ybl[t % 2]
                    ob = obl[t % 2]
                    yk = "sy0" if t % 2 == 0 else "sy1"
                    sk = "st0" if t % 2 == 0 else "st1"
                    w(dx, "gg", st10["gga"].get(t - 2, 0))
                    yl = inc(dx.dma_start(out=yb[:],
                                          in_=y_nat[row0:row0 + 128, c0:c0 + 1024]),
                             yk)
                    w(gg, yk, yl)
                    w(gg, "vv", natp_done)
                    w(gg, sk, st10["ost"].get(t - 2, 0))
                    st10["gga"][t] = inc(gg.tensor_add(
                        ob[:], natp[:, t * 1024:(t + 1) * 1024], yb[:]), "gg")
                    w(dx, "gg", st10["gga"][t])
                    st10["ost"][t] = inc(dx.dma_start(
                        out=out[row0:row0 + 128, c0:c0 + 1024], in_=ob[:]), sk)
                st10["natp_gg"] = st10["gga"][NT - 1]
                st10["ost"] = {}
                st10["gga"] = {}

        phase_A(0)
        phase_B(0)
        phase_A(1)
        phase_B(1)
        phase_C(0)
        phase_C(1)

        for e in (pe, vv, ss, dx):
            w(e, "st0")
            w(e, "st1")
            e.nop().then_inc(sems["fin"], 1)
        w(gg, "st0")
        w(gg, "st1")
        gg.wait_ge(sems["fin"], 4)
        for k in CKEYS + DKEYS:
            gg.sem_clear(sems[k])

    return nc


def kernel(**inputs):
    x = np.ascontiguousarray(np.asarray(inputs["L0_attn_hidden_states"],
                                        dtype=np.float32))
    wq = np.asarray(inputs["L0_attn_wq"], dtype=np.float32)
    wk = np.asarray(inputs["L0_attn_wk"], dtype=np.float32)
    wv = np.asarray(inputs["L0_attn_wv"], dtype=np.float32)
    wo = np.asarray(inputs["L0_attn_wo"], dtype=np.float32)
    gate = np.asarray(inputs["L0_ffn_gate_w"], dtype=np.float32)
    up = np.asarray(inputs["L0_ffn_up_w"], dtype=np.float32)
    down = np.asarray(inputs["L0_ffn_down_w"], dtype=np.float32)
    fc1 = np.asarray(inputs["L0_ffn_fc1_w"], dtype=np.float32)
    fc2 = np.asarray(inputs["L0_ffn_fc2_w"], dtype=np.float32)
    ex = np.asarray(inputs["L0_ffn_expert_w"], dtype=np.float32)
    gam = np.asarray(inputs["L0_ffn_ln_gamma"], dtype=np.float32)
    bet = np.asarray(inputs["L0_ffn_ln_beta"], dtype=np.float32)

    shared = {
        "wq_p": _panels(np.ascontiguousarray(wq.T)),
        "wk_p": _panels(np.ascontiguousarray(wk.T)),
        "wv_p": _panels(np.ascontiguousarray(wv.T)),
        "wo_p": _panels(np.ascontiguousarray(wo.T)),
        "up_p": _panels(up),
        "down_p": _panels(down),
        "fc1_p": _panels(fc1),
        "fc2_p": _panels(fc2),
        "ex_p": _panels(ex),
        "gamma_fm": np.ascontiguousarray(gam.reshape(32, 128).T),
        "beta_fm": np.ascontiguousarray(bet.reshape(32, 128).T),
        "gate_p": np.ascontiguousarray(
            gate.reshape(32, 128, 8).transpose(1, 0, 2).reshape(128, 256)),
    }
    shared.update(_host_consts())

    nc = build_core_program()
    in_maps = []
    for b in range(NCORES):
        m = {"x_in": np.ascontiguousarray(x[b])}
        m.update(shared)
        in_maps.append(m)
    res = run_bass_kernel_spmd(nc, in_maps, core_ids=list(range(NCORES)))
    outs = [res.results[b]["out"] for b in range(NCORES)]
    return np.stack(outs, axis=0).astype(np.float32)




# version 16
# speedup vs baseline: 1.1984x; 1.1984x over previous
import math
import numpy as np
import ml_dtypes

import concourse.bass as bass
import concourse.mybir as mybir
from concourse.bass_utils import run_bass_kernel_spmd

F32 = mybir.dt.float32
BF16 = mybir.dt.bfloat16
AF = mybir.ActivationFunctionType
ALU = mybir.AluOpType
AX = mybir.AxisListType

NCORES = 8
S = 2048
D = 4096
HD = 128
NH = 32
NKV = 8
GRP = NH // NKV
INT = 8192
LAT = 2048
HALF = 1024
NT = 8
NCH = 32
EPS = 1e-5
BND = 611
NEG = float(np.finfo(np.float32).min)


def _panels(w):
    Kd, Md = w.shape
    a = w.reshape(Kd // 128, 128, Md // 128, 128)
    a = a.transpose(2, 1, 0, 3)
    return np.ascontiguousarray(a.reshape(Md // 128, 128, Kd).astype(np.float32))


def _host_consts():
    inv = 1.0 / 10000.0 ** (np.arange(0, HD, 2, dtype=np.float64) / HD)
    pos = np.arange(S, dtype=np.float64)
    fr = pos[:, None] * inv[None, :]
    emb = np.concatenate([fr, fr], -1)
    cos = np.cos(emb).astype(np.float32).T
    sin = np.sin(emb).astype(np.float32).T
    sc = 1.0 / math.sqrt(HD)
    kpos = np.arange(S)
    ks = np.where(kpos < 35, 0.5, np.where(kpos < BND, 1.5, 1.0)).astype(np.float32)
    c = {}
    c["cosq"] = (cos * sc).astype(ml_dtypes.bfloat16)
    c["sinq"] = (sin * sc).astype(ml_dtypes.bfloat16)
    c["cosk"] = cos.astype(ml_dtypes.bfloat16)
    c["sink"] = sin.astype(ml_dtypes.bfloat16)
    c["ksbc"] = np.ascontiguousarray(np.broadcast_to(ks, (128, S))).astype(
        ml_dtypes.bfloat16)
    inv_ks = (1.0 / ks[:640]).astype(np.float32)
    c["invks"] = np.ascontiguousarray(np.broadcast_to(inv_ks, (128, 640)))
    m4 = np.where((np.arange(512, 640)[:, None] >= BND), 1.0, inv_ks[None, :])
    c["invt4"] = np.ascontiguousarray(m4.astype(np.float32))
    tri = np.where(np.arange(128)[:, None] >= np.arange(128)[None, :], 0.0, NEG)
    c["tri"] = tri.astype(np.float32)
    R = np.zeros((128, 128), dtype=np.float32)
    for i in range(64):
        R[i, i + 64] = -1.0
        R[i + 64, i] = 1.0
    c["rotT"] = np.ascontiguousarray(R.T).astype(ml_dtypes.bfloat16)
    c["idbf"] = np.eye(128, dtype=np.float32).astype(ml_dtypes.bfloat16)
    c["idf32"] = np.eye(128, dtype=np.float32)
    c["onesrow_bf"] = np.ones((1, 128), dtype=np.float32).astype(ml_dtypes.bfloat16)
    c["onesrow_f"] = np.ones((1, 128), dtype=np.float32)
    c["onescol"] = np.ones((128, 1), dtype=np.float32)
    c["onescol_bf"] = np.ones((128, 1), dtype=np.float32).astype(ml_dtypes.bfloat16)
    c["ones128"] = np.ones((128, 128), dtype=np.float32).astype(ml_dtypes.bfloat16)
    trik = np.arange(128)[:, None]
    triq = np.arange(512)[None, :] % 128
    c["tri4"] = np.ascontiguousarray(
        np.where(trik <= triq, 0.0, NEG).astype(np.float32)).astype(
        ml_dtypes.bfloat16)
    ks640 = np.where(np.arange(640) < 35, 0.5,
                     np.where(np.arange(640) < BND, 1.5, 1.0)).astype(np.float32)
    f4 = np.ones((128, 5 * 116), dtype=np.float32)
    for kc in range(5):
        kvec = ks640[kc * 128:(kc + 1) * 128]
        for a in range(4):
            f4[:, kc * 116 + a * 29: kc * 116 + (a + 1) * 29] = kvec[:, None]
    c["t4fT"] = np.ascontiguousarray(f4).astype(ml_dtypes.bfloat16)
    return c


def build_core_program(dbg=False):
    nc = bass.Bass("TRN2", target_bir_lowering=False, detect_race_conditions=False)
    I = {}

    def inp(name, shape, dt=F32):
        I[name] = nc.dram_tensor(name, shape, dt, kind="ExternalInput").ap()
        return I[name]

    x_in = inp("x_in", [S, D])
    wq_p = inp("wq_p", [32, 128, D])
    wk_p = inp("wk_p", [8, 128, D])
    wv_p = inp("wv_p", [8, 128, D])
    wo_p = inp("wo_p", [32, 128, D])
    up_p = inp("up_p", [64, 128, D])
    down_p = inp("down_p", [32, 128, INT])
    fc1_p = inp("fc1_p", [16, 128, D])
    fc2_p = inp("fc2_p", [32, 128, LAT])
    ex_p = inp("ex_p", [16, 128, LAT])
    gate_p = inp("gate_p", [128, 256])
    gamma_fm = inp("gamma_fm", [128, 32])
    beta_fm = inp("beta_fm", [128, 32])
    c_cosq = inp("cosq", [128, S], BF16)
    c_sinq = inp("sinq", [128, S], BF16)
    c_cosk = inp("cosk", [128, S], BF16)
    c_sink = inp("sink", [128, S], BF16)
    c_ksbc = inp("ksbc", [128, S], BF16)
    c_invks = inp("invks", [128, 640])
    c_invt4 = inp("invt4", [128, 640])
    c_tri = inp("tri", [128, 128])
    c_rotT = inp("rotT", [128, 128], BF16)
    c_idbf = inp("idbf", [128, 128], BF16)
    c_idf32 = inp("idf32", [128, 128])
    c_onesrow = inp("onesrow_bf", [1, 128], BF16)
    c_onesrowf = inp("onesrow_f", [1, 128])
    c_onescol = inp("onescol", [128, 1])
    c_onescolb = inp("onescol_bf", [128, 1], BF16)
    c_tri4 = inp("tri4", [128, 512], BF16)
    c_ones128 = inp("ones128", [128, 128], BF16)
    c_t4fT = inp("t4fT", [128, 580], BF16)

    out = nc.dram_tensor("out", [S, D], F32, kind="ExternalOutput").ap()

    kind_scr = "ExternalOutput" if dbg else "Internal"
    qsp = nc.dram_tensor("qsp", [16, 128, 32, 128], BF16, kind=kind_scr).ap()
    ao_nat = nc.dram_tensor("ao_nat", [S, D], BF16, kind=kind_scr).ap()
    y_nat = nc.dram_tensor("y_nat", [S, D], F32, kind=kind_scr).ap()
    s0sp = nc.dram_tensor("s0sp", [32, 128, S], F32, kind=kind_scr).ap()
    if dbg:
        d_xnT = nc.dram_tensor("d_xnT", [128, 32768], BF16, kind="ExternalOutput").ap()
        d_ksv = nc.dram_tensor("d_ksv", [128, 32768], BF16, kind="ExternalOutput").ap()
        d_attnT = nc.dram_tensor("d_attnT", [128, 32768], BF16,
                                 kind="ExternalOutput").ap()
        d_zst = nc.dram_tensor("d_zst", [2, 128, 32768], BF16,
                               kind="ExternalOutput").ap()
        d_sst = nc.dram_tensor("d_sst", [2, 128, 32768], BF16,
                               kind="ExternalOutput").ap()
        d_xlat = nc.dram_tensor("d_xlat", [128, 16384], BF16,
                                kind="ExternalOutput").ap()
        d_moe = nc.dram_tensor("d_moe", [128, 16384], BF16, kind="ExternalOutput").ap()
        d_rw = nc.dram_tensor("d_rw", [128, 1024], F32, kind="ExternalOutput").ap()
        d_bc = nc.dram_tensor("d_bc", [4, 128, 1024], F32, kind="ExternalOutput").ap()
        d_mom = nc.dram_tensor("d_mom", [128, 64], F32, kind="ExternalOutput").ap()
        d_cc = nc.dram_tensor("d_cc", [128, 8], F32, kind="ExternalOutput").ap()
        d_tm = nc.dram_tensor("d_tm", [128, 32], F32, kind="ExternalOutput").ap()

    from contextlib import ExitStack
    with ExitStack() as ctx:
        heap = ctx.enter_context(nc.sbuf_tensor("heap", [128, 180224],
                                                mybir.dt.uint8))

        def hv(off, shape, dt):
            nb = {F32: 4, BF16: 2}[dt]
            assert off + shape[1] * nb <= 180224, f"heap view oob: {off}"
            return heap.ap()[0:shape[0], off:off + shape[1] * nb].bitcast(dt)

        H0, H1, H2, H3 = 0, 65536, 131072, 163840

        sb = lambda n, s, d=F32: ctx.enter_context(nc.sbuf_tensor(n, s, d))
        p_cosq = sb("p_cosq", [128, S], BF16)
        p_sinq = sb("p_sinq", [128, S], BF16)
        p_cosk = sb("p_cosk", [128, S], BF16)
        p_sink = sb("p_sink", [128, S], BF16)
        p_ksbc = sb("p_ksbc", [128, S], BF16)
        p_invks = sb("p_invks", [128, 640])
        p_rotT = sb("p_rotT", [128, 128], BF16)
        p_idbf = sb("p_idbf", [128, 128], BF16)
        p_idf32 = sb("p_idf32", [128, 128])
        p_ones = sb("p_ones", [1, 128], BF16)
        p_onesf = sb("p_onesf", [1, 128])
        p_onescol = sb("p_onescol", [128, 1])
        p_onescolb = sb("p_onescolb", [128, 1], BF16)
        p_tri4 = sb("p_tri4", [128, 512], BF16)
        p_ones128 = sb("p_ones128", [128, 128], BF16)
        p_t4fT = sb("p_t4fT", [128, 580], BF16)
        p_gfm = sb("p_gfm", [128, 32])
        p_bfm = sb("p_bfm", [128, 32])
        p_gatep = sb("p_gatep", [128, 256], BF16)
        statsW = sb("statsW", [128, 128])
        bW = sb("bW", [128, 64])
        constrow = sb("constrow", [1, 16])
        constcol = sb("constcol", [128, 8])
        t128a = sb("t128a", [128, 1])
        t128b = sb("t128b", [128, 1])
        t128c = sb("t128c", [128, 1])
        t128d = sb("t128d", [128, 1])
        rwtop = sb("rwtop", [128, 8])
        rw_tm = sb("rw_tm", [128, 8])
        qsqs = sb("qsqs", [128, 256], BF16)
        km_blk = sb("km_blk", [1, 128])
        km_blkp = sb("km_blkp", [1, 40])
        kmg8b = sb("kmg8b", [1, 8])
        qm1 = sb("qm1", [1, 1])
        qm2 = sb("qm2", [1, 2])
        kmg8 = sb("kmg8", [1, 8])
        u2row = sb("u2row", [1, 8])
        urow_bf = sb("urow_bf", [1, 8], BF16)
        negU8 = sb("negU8", [128, 8])
        rsum = sb("rsum", [128, 1])
        p_eps = sb("p_eps", [128, 1])

        PBt = ctx.enter_context(nc.psum_tensor("PBp", [128, 2048], F32))
        PMt = ctx.enter_context(nc.psum_tensor("PMp", [128, 1024], F32))
        PTt = ctx.enter_context(nc.psum_tensor("PTp", [128, 2048], BF16))
        PB = PBt.ap()
        PM = PMt.ap()
        PT = PTt.ap()
        PTf = PT.bitcast(F32)

        sems = {}
        cnt = {}
        lastw = {}
        CKEYS = ("pe", "vv", "ss", "gg", "fin")
        DKEYS = ("cst", "cstg", "dwa", "dwb", "dwc", "dwd", "sxa", "sqs", "qb0", "qb1", "sna",
                 "sao", "sx1", "syw", "sv1", "s0a", "s0b", "sv2", "sy0", "sy1",
                 "st0", "st1", "dbg")
        for k in CKEYS + DKEYS:
            sems[k] = ctx.enter_context(nc.semaphore(k))
            cnt[k] = 0

        pe, vv, ss, gg, dx = nc.tensor, nc.vector, nc.scalar, nc.gpsimd, nc.sync
        EALL = (pe, vv, ss, gg, dx)

        def inc(instr, key, n=None):
            if n is None:
                n = 16 if key in DKEYS else 1
            instr.then_inc(sems[key], n)
            cnt[key] += n
            return cnt[key]

        def w(eng, key, val=None):
            if val is None:
                val = cnt[key]
            if val <= 0:
                return
            ek = (id(eng), key)
            if lastw.get(ek, -1) >= val:
                return
            eng.wait_ge(sems[key], val)
            lastw[ek] = val

        def barrier(*engines):
            for e in engines or EALL:
                for k in CKEYS[:-1] + DKEYS:
                    w(e, k)

        def vsync(instr):
            c = inc(instr, "vv")
            w(vv, "vv", c)
            return c

        gst = {"ptvv": 0, "pm_vv": 0, "ptp": 0, "ptvv2": [0, 0], "ptss2": [0, 0]}

        def pt_begin():
            par = gst["ptp"]
            gst["ptp"] = 1 - par
            w(pe, "vv", gst["ptvv2"][par])
            return PT[:, par * 1024:(par + 1) * 1024], par

        def pt_done(par, copy_cnt):
            gst["ptvv2"][par] = copy_cnt
            gst["ptvv"] = copy_cnt

        for dst, src in [
            (p_cosq, c_cosq), (p_sinq, c_sinq), (p_cosk, c_cosk), (p_sink, c_sink),
            (p_ksbc, c_ksbc), (p_invks, c_invks),
            (p_rotT, c_rotT), (p_idbf, c_idbf), (p_idf32, c_idf32),
            (p_ones, c_onesrow), (p_onesf, c_onesrowf), (p_onescol, c_onescol),
            (p_onescolb, c_onescolb), (p_tri4, c_tri4), (p_t4fT, c_t4fT),
            (p_ones128, c_ones128),
            (p_gfm, gamma_fm), (p_bfm, beta_fm),
        ]:
            inc(dx.dma_start(out=dst[:], in_=src), "cst")
        inc(gg.dma_start(out=p_gatep[:], in_=gate_p), "cstg")

        w(vv, "cst")
        if dbg:
            vv.memset(heap.ap()[:, :].bitcast(F32), 0.0)
        vv.memset(p_eps[:], EPS)
        vv.memset(statsW.ap()[:, 0:128:4], 1.0)
        vv.tensor_copy(statsW.ap()[:, 1:128:4], p_gfm[:])
        vv.tensor_mul(statsW.ap()[:, 2:128:4], p_gfm[:], p_gfm[:])
        vv.tensor_mul(statsW.ap()[:, 3:128:4], p_gfm[:], p_bfm[:])
        vv.tensor_copy(bW.ap()[:, 0:64:2], p_bfm[:])
        inc(vv.tensor_mul(bW.ap()[:, 1:64:2], p_bfm[:], p_bfm[:]), "vv")
        w(pe, "vv")
        cols = [(statsW, 4, 1), (statsW, 4, 2), (statsW, 4, 3), (bW, 2, 0), (bW, 2, 1)]
        for j, (mat, step, cofs) in enumerate(cols):
            for cc in range(32):
                mm = pe.matmul(PM[0:1, j:j + 1],
                               mat.ap()[:, cc * step + cofs: cc * step + cofs + 1],
                               p_onescol[:], start=(cc == 0), stop=(cc == 31))
        inc(mm, "pe")
        w(vv, "pe")
        vv.tensor_copy(constrow.ap()[0:1, 0:5], PM[0:1, 0:5])
        inc(vv.tensor_scalar(constrow.ap()[0:1, 8:13], constrow.ap()[0:1, 0:5],
                             1.0 / D, None, op0=ALU.mult), "vv")
        w(pe, "vv")
        mm = pe.matmul(PM[:, 0:5], p_onesf[:], constrow.ap()[0:1, 8:13],
                       start=True, stop=True)
        inc(mm, "pe")
        w(vv, "pe")
        inc(vv.tensor_copy(constcol.ap()[:, 0:5], PM[:, 0:5]), "vv")
        CGD, CG2D, CGBD, CBD, CB2D = (constcol.ap()[:, j:j + 1] for j in range(5))

        wpan = [hv(H3, [128, 4096], BF16), hv(H3 + 8192, [128, 4096], BF16)]
        wpan4 = [hv(H3 + i * 4096, [128, 2048], BF16) for i in range(4)]
        PAN = {"pe": 0}

        DWK = ("dwa", "dwb", "dwc", "dwd")

        def gemm(ms_count, cs_count, panels_ap, rhs_fn, evac_fn, wait_rhs=None,
                 pan_w=None):
            pw = pan_w if pan_w is not None else cs_count * 128
            hpw = pw // 2
            hcs = (cs_count + 1) // 2
            pe_after = {}
            psfree = {}
            tail = [None]
            dwcnt = {}

            def issue_panels(m):
                if m >= ms_count:
                    return
                for hpi in range(2):
                    hp = 2 * m + hpi
                    bi = hp % 4
                    gate_m = (hp - 4) // 2
                    if gate_m >= 0:
                        w(gg, "pe", pe_after[gate_m])
                    else:
                        w(gg, "pe", PAN["pe"])
                    lo = hpi * hpw
                    dwcnt[hp] = inc(gg.dma_start(
                        out=wpan4[bi][:, 0:pw - lo if hpi else hpw],
                        in_=panels_ap[m, :, lo:lo + (pw - lo if hpi else hpw)]),
                        DWK[bi])

            issue_panels(0)
            issue_panels(1)
            for m in range(ms_count):
                issue_panels(m + 1)
                if wait_rhs is not None:
                    wait_rhs(m)
                if m >= 2:
                    w(pe, psfree[m - 2][0], psfree[m - 2][1])
                ps = PB[:, 0:1024] if m % 2 == 0 else PB[:, 1024:2048]
                b0, b1 = (2 * m) % 4, (2 * m + 1) % 4
                for tc in range(2):
                    pv = ps[:, tc * 512:(tc + 1) * 512]
                    for cc in range(cs_count):
                        if cc == 0:
                            w(pe, DWK[b0], dwcnt[2 * m])
                        if cc == hcs:
                            w(pe, DWK[b1], dwcnt[2 * m + 1])
                        if cc < hcs:
                            lhs = wpan4[b0][:, cc * 128:(cc + 1) * 128]
                        else:
                            lhs = wpan4[b1][:, (cc - hcs) * 128:(cc - hcs + 1) * 128]
                        mm = pe.matmul(pv, lhs, rhs_fn(cc, tc),
                                       start=(cc == 0), stop=(cc == cs_count - 1))
                pe_after[m] = inc(mm, "pe")
                r = evac_fn(m, ps, pe_after[m])
                if isinstance(r[0], tuple):
                    psfree[m], newtail = r
                else:
                    psfree[m], newtail = r, None
                if tail[0] is not None:
                    tail[0]()
                tail[0] = newtail
            if tail[0] is not None:
                tail[0]()
            PAN["pe"] = pe_after[ms_count - 1]

        def ln_tm(x_ap, width, sq_discard_ap, out_ap):
            w(ss, "vv")
            vsync(vv.reduce_sum(t128a[:], x_ap, axis=AX.X))
            sq = inc(ss.activation(sq_discard_ap, x_ap, AF.Square,
                                   accum_out=t128b[:]), "ss")
            vsync(vv.tensor_scalar(t128a[:], t128a[:], 1.0 / width, None,
                                   op0=ALU.mult))
            w(vv, "ss", sq)
            vsync(vv.tensor_scalar(t128b[:], t128b[:], 1.0 / width, None,
                                   op0=ALU.mult))
            v1 = vsync(vv.scalar_tensor_tensor(t128d[:], t128a[:], t128a[:], t128b[:],
                                               op0=ALU.mult, op1=ALU.subtract))
            w(ss, "vv", v1)
            s1 = inc(ss.activation(t128c[:], t128d[:], AF.Sqrt, bias=p_eps[:], scale=-1.0),
                     "ss")
            w(vv, "ss", s1)
            vsync(vv.reciprocal(t128c[:], t128c[:]))
            v2 = vsync(vv.scalar_tensor_tensor(t128d[:], t128a[:], -1.0, t128c[:],
                                               op0=ALU.mult, op1=ALU.mult))
            return inc(vv.tensor_scalar(out_ap, x_ap, t128c[:], t128d[:],
                                        op0=ALU.mult, op1=ALU.add), "vv")

        xnT = hv(H0, [128, 32768], BF16)
        KsV = hv(H1, [128, 32768], BF16)
        Ks = KsV[:, 0:16384]
        Vst = KsV[:, 16384:32768]
        stgA2 = [hv(H2, [128, 4096], BF16), hv(H2 + 8192, [128, 4096], BF16)]
        q_sb2 = [hv(H2, [128, 1024], BF16), hv(H2 + 2048, [128, 1024], BF16)]
        rot_bf = hv(H2 + 4096, [128, 1024], BF16)
        rope_t1 = hv(H2 + 6144, [128, 1024], BF16)
        xbuf = hv(H2 + 16384, [128, 4096], F32)

        st_a = {"apply": [0, 0], "tp": [0, 0], "qsb_pe": [0, 0], "qsb_gg": [0, 0],
                "t1_vv": 0, "rot_dma": {}}

        def phase_A(h):
            barrier(dx, ss, vv, pe, gg)
            for t in range(NT):
                row0 = h * HALF + t * 128
                stgA = stgA2[t % 2]
                w(dx, "vv", st_a["apply"][0])
                ld = inc(dx.dma_start(out=xbuf[:], in_=x_in[row0:row0 + 128, :]),
                         "sxa")
                w(vv, "sxa", ld)
                w(ss, "sxa", ld)
                w(ss, "pe", st_a["tp"][t % 2])
                w(vv, "pe", st_a["tp"][t % 2])
                ap = ln_tm(xbuf[:], D, stgA[:], stgA[:])
                st_a["apply"] = [ap, st_a["apply"][0]]
                w(pe, "vv", ap)
                for cg in range(4):
                    PTv, par = pt_begin()
                    for j in range(8):
                        cc = cg * 8 + j
                        tp = pe.transpose(PTv[:, j * 128:(j + 1) * 128],
                                          stgA[:, cc * 128:(cc + 1) * 128], p_idbf[:])
                    st_a["tp"][t % 2] = inc(tp, "pe")
                    w(vv, "pe", st_a["tp"][t % 2])
                    dst = xnT.rearrange("p (c r) -> p c r", c=32, r=1024)[
                        :, cg * 8:(cg + 1) * 8, t * 128:(t + 1) * 128]
                    pt_done(par, inc(vv.tensor_copy(
                        dst, PTv.rearrange("p (c r) -> p c r", c=8, r=128)), "vv"))
            xn_ready = cnt["vv"]
            barrier(ss, gg)

            def rhs_xn(ccv, tc):
                return xnT[:, ccv * 1024 + tc * 512: ccv * 1024 + (tc + 1) * 512]

            def wait_rhs(m):
                w(pe, "vv", xn_ready)

            csq = p_cosq.ap()[:, h * HALF:(h + 1) * HALF]
            snq = p_sinq.ap()[:, h * HALF:(h + 1) * HALF]
            csk = p_cosk.ap()[:, h * HALF:(h + 1) * HALF]
            snk = p_sink.ap()[:, h * HALF:(h + 1) * HALF]
            ksl = p_ksbc.ap()[:, h * HALF:(h + 1) * HALF]

            def mk_evac_rope(cs_ap, sn_ap, store_fn, dma_guard):
                def evac(m, ps, pe_cnt):
                    q_sb = q_sb2[m % 2]
                    w(ss, "pe", max(pe_cnt, st_a["qsb_pe"][m % 2]))
                    w(ss, "vv", st_a["qsb_gg"][m % 2])
                    scopy = inc(ss.activation(q_sb[:], ps, AF.Copy), "ss")

                    def tail():
                        w(pe, "ss", scopy)
                        w(pe, "vv", gst["pm_vv"])
                        pe.matmul(PM[:, 0:512], p_rotT[:], q_sb[:, 0:512],
                                  start=True, stop=True)
                        mm = pe.matmul(PM[:, 512:1024], p_rotT[:], q_sb[:, 512:1024],
                                       start=True, stop=True)
                        rotc = inc(mm, "pe")
                        st_a["qsb_pe"][m % 2] = rotc
                        w(vv, "pe", rotc)
                        if dma_guard:
                            w(vv, "sqs", st_a["rot_dma"].get(m - 1, 0))
                        gst["pm_vv"] = inc(vv.tensor_copy(rot_bf[:], PM[:, 0:1024]),
                                           "vv")
                        w(vv, "ss", scopy)
                        st_a["qsb_gg"][m % 2] = inc(
                            vv.tensor_tensor(rope_t1[:], q_sb[:], cs_ap, op=ALU.mult),
                            "vv")
                        vv.tensor_mul(rot_bf[:], rot_bf[:], sn_ap)
                        qv = inc(vv.tensor_add(rot_bf[:], rope_t1[:], rot_bf[:]),
                                 "vv")
                        st_a["t1_vv"] = qv
                        store_fn(m, qv)

                    return ("ss", scopy), tail
                return evac

            def store_q(m, qv):
                w(dx, "vv", qv)
                dstq = qsp[h * 8:(h + 1) * 8, :, m, :].rearrange("a p j -> p a j")
                st_a["rot_dma"][m] = inc(
                    dx.dma_start(out=dstq,
                                 in_=rot_bf[:].rearrange("p (a j) -> p a j", a=8)),
                    "sqs")

            def store_k(m, qv):
                dstk = Ks[:, m * 2048 + h * HALF: m * 2048 + (h + 1) * HALF]
                inc(vv.tensor_tensor(dstk, rot_bf[:], ksl, op=ALU.mult), "vv")

            gemm(32, 32, wq_p, rhs_xn, mk_evac_rope(csq, snq, store_q, True),
                 wait_rhs=wait_rhs)
            st_a["rot_dma"] = {}
            gemm(8, 32, wk_p, rhs_xn, mk_evac_rope(csk, snk, store_k, False),
                 wait_rhs=wait_rhs)

            def evac_v(m, ps, pe_cnt):
                q_sb = q_sb2[m % 2]
                w(ss, "pe", max(pe_cnt, st_a["qsb_pe"][m % 2]))
                w(ss, "vv", st_a["qsb_gg"][m % 2])
                scopy = inc(ss.activation(q_sb[:], ps, AF.Copy), "ss")

                def tail():
                    w(pe, "ss", scopy)
                    PTv, par = pt_begin()
                    for j in range(8):
                        tp = pe.transpose(PTv[:, j * 128:(j + 1) * 128],
                                          q_sb[:, j * 128:(j + 1) * 128], p_idbf[:])
                    tpc = inc(tp, "pe")
                    st_a["qsb_pe"][m % 2] = tpc
                    w(vv, "pe", tpc)
                    dstv = Vst.rearrange("p (a r) -> p a r", a=16, r=1024)[
                        :, h * 8:(h + 1) * 8, m * 128:(m + 1) * 128]
                    pt_done(par, inc(vv.tensor_copy(
                        dstv, PTv.rearrange("p (a r) -> p a r", a=8, r=128)), "vv"))

                return ("ss", scopy), tail

            gemm(8, 32, wv_p, rhs_xn, evac_v, wait_rhs=wait_rhs)
            if dbg and h == 0:
                barrier(dx)
                inc(dx.dma_start(out=d_xnT, in_=xnT[:]), "dbg")
                inc(dx.dma_start(out=d_ksv, in_=KsV[:]), "dbg")

        attnT = hv(H0, [128, 32768], BF16)
        PTst = hv(H2, [128, 8192], BF16)
        qblk = [hv(H2 + 16384, [128, 4096], BF16), hv(H2 + 24576, [128, 4096], BF16)]
        Kpfx = hv(H3, [128, 5120], BF16)
        sqscr = hv(H3 + 10240, [128, 2048], BF16)
        rsbf = hv(H3 + 10240, [1, 512], F32)
        rrsb = hv(H3 + 14336, [128, 512], F32)

        def phase_B(h):
            barrier(dx, pe, vv, ss, gg)
            st_b = {"av_vv": [0, 0], "expfree": [0, 0, 0, 0],
                    "pvprev": {}, "sumfree": [0, 0]}
            kpfx_done = 0
            if h == 0:
                for m in range(8):
                    kpfx_done = inc(vv.tensor_tensor(
                        Kpfx[:, m * 640:(m + 1) * 640],
                        Ks[:, m * 2048: m * 2048 + 640],
                        p_invks.ap()[:, 0:640], op=ALU.mult), "vv")
            nkey = (h + 1) * 1024
            kb_done = 0
            for g in range(8):
                sqc = inc(vv.tensor_mul(sqscr[:, 0:nkey],
                                        Ks[:, g * 2048: g * 2048 + nkey],
                                        Ks[:, g * 2048: g * 2048 + nkey]), "vv")
                w(pe, "vv", cnt["vv"])
                for c4 in range(0, nkey, 512):
                    mm = pe.matmul(PB[0:1, c4:c4 + 512], p_onescolb[:],
                                   sqscr[:, c4:c4 + 512], start=True, stop=True)
                mmn = inc(mm, "pe")
                w(vv, "pe", mmn)
                kb_done = vsync(vv.reduce_max(
                    km_blk.ap()[0:1, g * 16: g * 16 + nkey // 128],
                    PB[0:1, 0:nkey].rearrange("p (a r) -> p a r", a=nkey // 128),
                    axis=AX.X))
            if h == 0:
                for g in range(8):
                    sqp = inc(vv.tensor_mul(sqscr[:, 0:640],
                                            Kpfx[:, g * 640:(g + 1) * 640],
                                            Kpfx[:, g * 640:(g + 1) * 640]),
                              "vv")
                    w(pe, "vv", cnt["vv"])
                    for c4, wdt in ((0, 512), (512, 128)):
                        mm = pe.matmul(PB[0:1, c4:c4 + wdt], p_onescolb[:],
                                       sqscr[:, c4:c4 + wdt],
                                       start=True, stop=True)
                    mmn = inc(mm, "pe")
                    w(vv, "pe", mmn)
                    kb_done = vsync(vv.reduce_max(
                        km_blkp.ap()[0:1, g * 5: g * 5 + 5],
                        PB[0:1, 0:640].rearrange("p (a r) -> p a r", a=5),
                        axis=AX.X))
            dmaq = {}
            for pre in range(2):
                dmaq[pre] = inc(dx.dma_start(
                    out=qblk[pre][:], in_=qsp[h * NT + pre]),
                    "qb0" if pre % 2 == 0 else "qb1")
            for t in range(NT):
                T = h * NT + t
                Wk = (T + 1) * 128
                nkc = T + 1
                qb = qblk[t % 2]
                w(pe, "qb0" if t % 2 == 0 else "qb1", dmaq[t])
                w(pe, "vv", kb_done)
                for par4 in range(4):
                    w(pe, "ss", st_b["expfree"][par4])
                w(vv, "qb0" if t % 2 == 0 else "qb1", dmaq[t])
                mm_of = {}
                for rnd in range(2):
                    for hh in range(16):
                        hd2 = rnd * 16 + hh
                        qs2 = qsqs.ap()[:, (hd2 % 2) * 128:(hd2 % 2) * 128 + 128]
                        w(vv, "pe", mm_of.get(hd2 - 2, 0))
                        sq = inc(vv.tensor_mul(
                            qs2, qb[:, hd2 * 128:(hd2 + 1) * 128],
                            qb[:, hd2 * 128:(hd2 + 1) * 128]), "vv")
                        w(pe, "vv", sq)
                        mm_of[hd2] = inc(
                            pe.matmul(PB[0:1, hh * 128:(hh + 1) * 128],
                                      p_onescolb[:], qs2,
                                      start=True, stop=True), "pe")
                    w(vv, "pe", cnt["pe"])
                    vsync(vv.reduce_max(qm2.ap()[0:1, rnd:rnd + 1],
                                        PB[0:1, 0:2048], axis=AX.X))
                    if rnd == 0:
                        w(pe, "vv", cnt["vv"])
                vsync(vv.reduce_max(qm1.ap()[0:1, 0:1],
                                    qm2.ap()[0:1, 0:2], axis=AX.X))
                if T <= 3:
                    for g3 in range(8):
                        rl = inc(vv.reduce_max(
                            kmg8.ap()[0:1, g3:g3 + 1],
                            km_blkp.ap()[0:1, g3 * 5: g3 * 5 + nkc],
                            axis=AX.X), "vv")
                    w(vv, "vv", rl)
                elif T == 4:
                    for g3 in range(8):
                        inc(vv.reduce_max(
                            kmg8.ap()[0:1, g3:g3 + 1],
                            km_blkp.ap()[0:1, g3 * 5: g3 * 5 + 5],
                            axis=AX.X), "vv")
                        rl = inc(vv.reduce_max(
                            kmg8b.ap()[0:1, g3:g3 + 1],
                            km_blk.ap()[0:1, g3 * 16: g3 * 16 + 5],
                            axis=AX.X), "vv")
                    w(vv, "vv", rl)
                    vsync(vv.tensor_tensor(kmg8.ap()[0:1, 0:8],
                                           kmg8.ap()[0:1, 0:8],
                                           kmg8b.ap()[0:1, 0:8], op=ALU.max))
                else:
                    for g3 in range(8):
                        rl = inc(vv.reduce_max(
                            kmg8.ap()[0:1, g3:g3 + 1],
                            km_blk.ap()[0:1, g3 * 16: g3 * 16 + nkc],
                            axis=AX.X), "vv")
                    w(vv, "vv", rl)
                vsync(vv.tensor_scalar_mul(u2row.ap()[0:1, 0:8],
                                           kmg8.ap()[0:1, 0:8],
                                           qm1.ap()[0:1, 0:1]))
                w(ss, "vv", cnt["vv"])
                sqd = inc(ss.activation(u2row.ap()[0:1, 0:8],
                                        u2row.ap()[0:1, 0:8], AF.Sqrt), "ss")
                w(vv, "ss", sqd)
                vsync(vv.tensor_scalar_mul(urow_bf.ap()[0:1, 0:8],
                                           u2row.ap()[0:1, 0:8], -1.0))
                w(pe, "vv", cnt["vv"])
                mmr = inc(pe.matmul(PB[:, 0:8], p_ones[:],
                                    urow_bf.ap()[0:1, 0:8],
                                    start=True, stop=True), "pe")
                w(vv, "pe", mmr)
                negu_done = vsync(vv.tensor_copy(negU8[:], PB[:, 0:8]))

                for g in range(8):
                    avreg = PM[:, 0:512] if g % 2 == 0 else PM[:, 512:1024]
                    rhsq = qb[:, g * 512:(g + 1) * 512]
                    expd = {}
                    pvnew = {}

                    def cons(kc, g=g, avreg=avreg, expd=expd, pvnew=pvnew):
                        w(pe, "ss", expd[kc])
                        if kc == 0:
                            w(pe, "vv", st_b["sumfree"][g % 2])
                            w(pe, "vv", st_b["av_vv"][g % 2])
                        pe.matmul(PTf[:, (g % 2) * 512:(g % 2) * 512 + 512],
                                  p_ones128[:],
                                  PTst[:, kc * 512:(kc + 1) * 512],
                                  start=(kc == 0), stop=(kc == nkc - 1))
                        pvnew[kc] = inc(pe.matmul(
                            avreg,
                            Vst[:, kc * 1024 + g * 128: kc * 1024 + (g + 1) * 128],
                            PTst[:, kc * 512:(kc + 1) * 512],
                            start=(kc == 0), stop=(kc == nkc - 1)), "pe")

                    pend2 = None
                    for kc in range(nkc):
                        par = kc % 4
                        reg = PB[:, par * 512:(par + 1) * 512]
                        w(pe, "ss", st_b["expfree"][par])
                        w(pe, "vv", negu_done)
                        if T <= 4:
                            w(pe, "vv", kpfx_done)
                            lhsk = Kpfx[:, g * 640 + kc * 128:
                                        g * 640 + (kc + 1) * 128]
                        else:
                            lhsk = Ks[:, g * 2048 + kc * 128:
                                      g * 2048 + (kc + 1) * 128]
                        mmS = inc(pe.matmul(reg, lhsk, rhsq,
                                            start=True, stop=True), "pe")
                        vv_pre = 0
                        if T == 4:
                            w(vv, "pe", mmS)
                            r3 = reg.rearrange("p (a r) -> p a r", a=4,
                                               r=128)[:, :, 99:128]
                            vv_pre = inc(vv.tensor_tensor(
                                r3, r3,
                                p_t4fT.ap()[:, kc * 116:(kc + 1) * 116].rearrange(
                                    "p (a r) -> p a r", a=4, r=29),
                                op=ALU.mult), "vv")
                        if kc == nkc - 1:
                            w(vv, "pe", mmS)
                            vv_pre = inc(vv.tensor_add(reg, reg, p_tri4[:]),
                                         "vv")
                        w(ss, "pe", mmS)
                        if vv_pre:
                            w(ss, "vv", vv_pre)
                        w(ss, "pe", st_b["pvprev"].get(kc, 0))
                        expd[kc] = inc(ss.activation(
                            PTst[:, kc * 512:(kc + 1) * 512], reg, AF.Exp,
                            bias=negU8.ap()[:, g:g + 1]), "ss")
                        st_b["expfree"][par] = expd[kc]
                        if pend2 is not None:
                            cons(pend2)
                        pend2 = kc
                    cons(pend2)
                    st_b["pvprev"] = pvnew
                    pvd = cnt["pe"]
                    w(vv, "pe", pvd)
                    rc = vsync(vv.reciprocal(
                        rrsb[:, 0:512],
                        PTf[:, (g % 2) * 512:(g % 2) * 512 + 512]))
                    st_b["sumfree"][g % 2] = rc
                    dsta = attnT.rearrange("p (a r) -> p a r", a=32, r=1024)[
                        :, 4 * g:4 * (g + 1), t * 128:(t + 1) * 128]
                    st_b["av_vv"][g % 2] = inc(vv.tensor_tensor(
                        dsta, avreg.rearrange("p (a r) -> p a r", a=4, r=128),
                        rrsb[:, 0:512].rearrange("p (a r) -> p a r", a=4, r=128),
                        op=ALU.mult), "vv")
                tile_pe = cnt["pe"]
                if t + 2 < NT:
                    w(dx, "pe", tile_pe)
                    dmaq[t + 2] = inc(dx.dma_start(
                        out=qblk[t % 2][:], in_=qsp[h * NT + t + 2]),
                        "qb0" if t % 2 == 0 else "qb1")
            attn_done = cnt["vv"]
            gst["ptvv2"] = [st_b["sumfree"][0], st_b["sumfree"][1]]

            barrier(gg, dx)
            ao_f32 = hv(H2, [128, 1024], F32)
            ao_bf2 = [hv(H2 + 4096, [128, 1024], BF16),
                      hv(H2 + 6144, [128, 1024], BF16)]
            natstg = hv(H2 + 8192, [128, 1024], BF16)
            st_wo = {"cast_gg": 0, "tp": 0, "nat_vv": 0, "nat_dx": 0}

            def rhs_attn(ccv, tc):
                return attnT[:, ccv * 1024 + tc * 512: ccv * 1024 + (tc + 1) * 512]

            def wait_rhs_wo(m):
                w(pe, "vv", attn_done)

            def evac_wo(m, ps, pe_cnt):
                w(vv, "pe", pe_cnt)
                w(vv, "ss", st_wo["cast_gg"])
                aoc = inc(vv.tensor_copy(ao_f32[:], ps), "vv")
                w(ss, "vv", aoc)
                w(ss, "pe", st_wo["tp"])
                ao_bf = ao_bf2[m % 2]
                cast = inc(ss.activation(ao_bf[:], ao_f32[:], AF.Copy), "ss")
                st_wo["cast_gg"] = cast

                def tail():
                    w(pe, "ss", cast)
                    w(pe, "vv", st_wo["nat_vv"])
                    PTv, par = pt_begin()
                    for j in range(8):
                        tp = pe.transpose(PTv[:, j * 128:(j + 1) * 128],
                                          ao_bf[:, j * 128:(j + 1) * 128], p_idbf[:])
                    st_wo["tp"] = inc(tp, "pe")
                    w(vv, "pe", st_wo["tp"])
                    w(vv, "sna", st_wo["nat_dx"])
                    st_wo["nat_vv"] = inc(vv.tensor_copy(natstg[:], PTv), "vv")
                    pt_done(par, st_wo["nat_vv"])
                    w(dx, "vv", st_wo["nat_vv"])
                    dstn = ao_nat[h * HALF:(h + 1) * HALF,
                                  m * 128:(m + 1) * 128].rearrange(
                        "(a p) j -> p a j", p=128)
                    st_wo["nat_dx"] = inc(
                        dx.dma_start(out=dstn,
                                     in_=natstg[:].rearrange("p (a j) -> p a j",
                                                             a=8)), "sna")

                return ("vv", aoc), tail

            gemm(32, 32, wo_p, rhs_attn, evac_wo, wait_rhs=wait_rhs_wo)
            if dbg and h == 0:
                barrier(dx)
                inc(dx.dma_start(out=d_attnT, in_=attnT[:]), "dbg")

        def phase_C(h):
            barrier(dx, vv, ss, pe, gg)
            zst = hv(H1, [128, 32768], BF16)
            aot = hv(H0, [128, 4096], BF16)
            xt = hv(H0 + 8192, [128, 4096], F32)
            ant = hv(H0 + 24576, [128, 4096], F32)
            zn = hv(H0 + 40960, [128, 4096], BF16)
            sqd = hv(H0 + 49152, [128, 4096], BF16)
            st_c = {"aol_free": 0, "tp": 0, "ystore": 0, "zapply": 0, "sq_last": 0}
            for t in range(NT):
                row0 = h * HALF + t * 128
                w(dx, "vv", st_c["zapply"])
                aol = inc(dx.dma_start(out=aot[:], in_=ao_nat[row0:row0 + 128, :]),
                          "sao")
                w(dx, "vv", st_c["aol_free"])
                xl = inc(dx.dma_start(out=xt[:], in_=x_in[row0:row0 + 128, :]), "sx1")
                w(vv, "sao", aol)
                w(ss, "sao", aol)
                w(ss, "pe", st_c["tp"])
                w(ss, "vv")
                vsync(vv.reduce_sum(t128a[:], aot[:], axis=AX.X))
                sq = inc(ss.activation(sqd[:], aot[:], AF.Square, accum_out=t128b[:]),
                         "ss")
                vsync(vv.tensor_scalar(t128a[:], t128a[:], 1.0 / D, None,
                                       op0=ALU.mult))
                w(vv, "ss", sq)
                vsync(vv.tensor_scalar(t128b[:], t128b[:], 1.0 / D, None,
                                       op0=ALU.mult))
                v1 = vsync(vv.scalar_tensor_tensor(t128d[:], t128a[:], t128a[:],
                                                   t128b[:], op0=ALU.mult,
                                                   op1=ALU.subtract))
                w(ss, "vv", v1)
                s1 = inc(ss.activation(t128c[:], t128d[:], AF.Sqrt, bias=p_eps[:],
                                       scale=-1.0), "ss")
                w(vv, "ss", s1)
                vsync(vv.reciprocal(t128c[:], t128c[:]))
                v2 = vsync(vv.scalar_tensor_tensor(t128d[:], t128a[:], -1.0, t128c[:],
                                                   op0=ALU.mult, op1=ALU.mult))
                ap2 = inc(vv.tensor_scalar(ant[:], aot[:], t128c[:], t128d[:],
                                           op0=ALU.mult, op1=ALU.add), "vv")
                w(vv, "sx1", xl)
                yv = inc(vv.tensor_add(ant[:], ant[:], xt[:]), "vv")
                st_c["aol_free"] = yv
                w(dx, "vv", yv)
                st_c["ystore"] = inc(dx.dma_start(out=y_nat[row0:row0 + 128, :],
                                                  in_=ant[:]), "syw")
                st_c["zapply"] = ln_tm(ant[:], D, sqd[:], zn[:])
                w(pe, "vv", st_c["zapply"])
                for cg in range(4):
                    PTv, par = pt_begin()
                    for j in range(8):
                        cc = cg * 8 + j
                        tp = pe.transpose(PTv[:, j * 128:(j + 1) * 128],
                                          zn[:, cc * 128:(cc + 1) * 128], p_idbf[:])
                    st_c["tp"] = inc(tp, "pe")
                    w(vv, "pe", st_c["tp"])
                    dstz = zst.rearrange("p (c r) -> p c r", c=32, r=1024)[
                        :, cg * 8:(cg + 1) * 8, t * 128:(t + 1) * 128]
                    pt_done(par, inc(vv.tensor_copy(
                        dstz, PTv.rearrange("p (c r) -> p c r", c=8, r=128)), "vv"))
                w(vv, "syw", st_c["ystore"])
            if dbg:
                w(dx, "vv")
                inc(dx.dma_start(out=d_zst[h], in_=zst[:]), "dbg")
            z_ready = cnt["vv"]

            def rhs_z(ccv, tc):
                return zst[:, ccv * 1024 + tc * 512: ccv * 1024 + (tc + 1) * 512]

            def wait_z(m):
                w(pe, "vv", z_ready)

            barrier(pe)
            w(pe, "cstg", 16)
            for t in range(NT):
                for cc in range(NCH):
                    mm = pe.matmul(PM[:, t * 8:(t + 1) * 8],
                                   zst[:, cc * 1024 + t * 128: cc * 1024 + (t + 1) * 128],
                                   p_gatep[:, cc * 8:(cc + 1) * 8],
                                   start=(cc == 0), stop=(cc == NCH - 1))
            gmm = inc(mm, "pe")
            w(vv, "pe", gmm)
            for t in range(NT):
                vsync(vv.max(out=rwtop[:], in_=PM[:, t * 8:(t + 1) * 8]))
                vsync(vv.tensor_add(rw_tm.ap()[:, t:t + 1], rwtop.ap()[:, 0:1],
                                    rwtop.ap()[:, 1:2]))
            w(pe, "vv")
            for t in range(NT):
                tp = pe.transpose(PB[0:1, t * 128:(t + 1) * 128],
                                  rw_tm.ap()[:, t:t + 1], p_idf32[:])
            inc(tp, "pe")
            w(vv, "pe")
            rw_row = hv(H2, [1, 1024], BF16)
            inc(vv.tensor_copy(rw_row[0:1, :], PB[0:1, 0:1024]), "vv")
            w(pe, "vv")
            pe.matmul(PM[:, 0:512], p_ones[:], rw_row[0:1, 0:512], start=True,
                      stop=True)
            mm = pe.matmul(PM[:, 512:1024], p_ones[:], rw_row[0:1, 512:1024],
                           start=True, stop=True)
            inc(mm, "pe")
            rw_bc = hv(H2 + 2048, [128, 1024], F32)
            w(vv, "pe")
            inc(vv.tensor_copy(rw_bc[:], PM[:, 0:1024]), "vv")
            if dbg and h == 0:
                w(dx, "vv")
                inc(dx.dma_start(out=d_rw, in_=rw_bc[:]), "dbg")

            xlat = hv(H0, [128, 16384], BF16)
            barrier(pe)

            def evac_fc1(m, ps, pe_cnt):
                w(vv, "pe", pe_cnt)
                c1 = inc(vv.scalar_tensor_tensor(xlat[:, m * 1024:(m + 1) * 1024],
                                                 ps, 1.0, rw_bc[:],
                                                 op0=ALU.mult, op1=ALU.mult), "vv")
                return ("vv", c1)

            gemm(16, 32, fc1_p, rhs_z, evac_fc1, wait_rhs=wait_z)
            xlat_ready = cnt["vv"]

            barrier(pe)
            moeT = hv(H2, [128, 16384], BF16)

            def rhs_xlat(ccv, tc):
                return xlat[:, ccv * 1024 + tc * 512: ccv * 1024 + (tc + 1) * 512]

            def wait_xlat(m):
                w(pe, "vv", xlat_ready)

            def evac_ex(m, ps, pe_cnt):
                w(vv, "pe", pe_cnt)
                c1 = inc(vv.tensor_copy(moeT[:, m * 1024:(m + 1) * 1024], ps), "vv")
                return ("vv", c1)

            gemm(16, 16, ex_p, rhs_xlat, evac_ex, wait_rhs=wait_xlat)
            moe_ready = cnt["vv"]
            if dbg and h == 0:
                w(dx, "vv", moe_ready)
                inc(dx.dma_start(out=d_xlat, in_=xlat[:]), "dbg")
                inc(dx.dma_start(out=d_moe, in_=moeT[:]), "dbg")

            barrier(pe)
            sevac = hv(H0 + 32768, [128, 1024], F32)
            st_f = {"sev_dx": 0}

            def rhs_moe(ccv, tc):
                return moeT[:, ccv * 1024 + tc * 512: ccv * 1024 + (tc + 1) * 512]

            def wait_moe(m):
                w(pe, "vv", moe_ready)

            def evac_fc2(m, ps, pe_cnt):
                w(vv, "pe", pe_cnt)
                w(vv, "sv1", st_f["sev_dx"])
                sv = inc(vv.tensor_copy(sevac[:], ps), "vv")
                w(dx, "vv", sv)
                st_f["sev_dx"] = inc(dx.dma_start(
                    out=s0sp[m, :, h * HALF:(h + 1) * HALF], in_=sevac[:]), "sv1")
                return ("vv", sv)

            gemm(32, 16, fc2_p, rhs_moe, evac_fc2, wait_rhs=wait_moe, pan_w=2048)
            fc2_stores = cnt["sv1"]

            barrier(pe, vv, ss)
            w(vv, "sv1", fc2_stores)
            hst = hv(H0, [128, 32768], BF16)
            relu_s = hv(H2, [128, 1024], BF16)
            st_up = {"relu_vv": 0}

            def evac_up(m, ps, pe_cnt):
                w(ss, "pe", pe_cnt)
                w(ss, "vv", st_up["relu_vv"])
                sd = inc(ss.activation(relu_s[:], ps, AF.Relu), "ss")
                w(vv, "ss", sd)
                st_up["relu_vv"] = inc(vv.tensor_mul(hst[:, m * 1024:(m + 1) * 1024],
                                                     relu_s[:], relu_s[:]), "vv")
                return ("ss", sd)

            gemm(32, 32, up_p, rhs_z, evac_up, wait_rhs=wait_z)
            hA_ready = cnt["vv"]

            def rhs_h(ccv, tc):
                return hst[:, ccv * 1024 + tc * 512: ccv * 1024 + (tc + 1) * 512]

            barrier(pe, vv, dx)
            s0buf = [hv(H2 + 4096, [128, 1024], F32), hv(H2 + 8192, [128, 1024], F32)]
            sevac2 = hv(H2 + 12288, [128, 1024], F32)
            w(dx, "sv1", fc2_stores)
            dma_s0 = {}
            for m0 in range(2):
                dma_s0[m0] = inc(dx.dma_start(
                    out=s0buf[m0][:], in_=s0sp[m0, :, h * HALF:(h + 1) * HALF]),
                    "s0a" if m0 % 2 == 0 else "s0b")
            st_d = {"sev_dx": 0}
            pe_after = {}
            psfree = {}
            dwc6 = {}

            def issue_c6(m):
                if m >= 32:
                    return
                w(gg, "pe", pe_after[m - 2] if m >= 2 else PAN["pe"])
                dk6 = "dwa" if m % 2 == 0 else "dwb"
                dwc6[m] = inc(gg.dma_start(out=wpan[m % 2][:],
                                           in_=down_p[m, :, 0:4096]), dk6)

            issue_c6(0)
            issue_c6(1)
            for m in range(32):
                issue_c6(m + 1)
                dk = "dwa" if m % 2 == 0 else "dwb"
                w(pe, dk, dwc6[m])
                w(pe, "vv", hA_ready)
                if m >= 2:
                    w(pe, "vv", psfree[m - 2])
                ps = PB[:, 0:1024] if m % 2 == 0 else PB[:, 1024:2048]
                for tc in range(2):
                    pv = ps[:, tc * 512:(tc + 1) * 512]
                    for cc in range(32):
                        mm = pe.matmul(pv, wpan[m % 2][:, cc * 128:(cc + 1) * 128],
                                       rhs_h(cc, tc), start=(cc == 0), stop=(cc == 31))
                pe_after[m] = inc(mm, "pe")
                w(vv, "pe", pe_after[m])
                w(vv, "s0a" if m % 2 == 0 else "s0b", dma_s0[m])
                w(vv, "sv2", st_d["sev_dx"])
                sv = inc(vv.tensor_add(sevac2[:], ps, s0buf[m % 2][:]), "vv")
                psfree[m] = sv
                if m + 2 < 32:
                    w(dx, "vv", sv)
                    dma_s0[m + 2] = inc(dx.dma_start(
                        out=s0buf[m % 2][:],
                        in_=s0sp[m + 2, :, h * HALF:(h + 1) * HALF]),
                        "s0a" if m % 2 == 0 else "s0b")
                w(dx, "vv", sv)
                st_d["sev_dx"] = inc(dx.dma_start(
                    out=s0sp[m, :, h * HALF:(h + 1) * HALF], in_=sevac2[:]), "sv2")
            downA_done = pe_after[31]
            downA_stores = cnt["sv2"]

            barrier(pe, ss, vv)

            def wait_z_b(m):
                w(pe, "vv", z_ready)
                w(pe, "pe", downA_done)

            gemm(32, 32, up_p[32:64], rhs_z, evac_up, wait_rhs=wait_z_b)
            hB_ready = cnt["vv"]

            barrier(pe, vv, ss, gg, dx)
            w(dx, "sv2", downA_stores)
            sst = hv(H1, [128, 32768], BF16)
            sqst2 = [hv(H2 + 16384, [128, 1024], F32), hv(H2 + 28672, [128, 1024], F32)]
            s_f32b = [hv(H2 + 20480, [128, 1024], F32), hv(H2 + 24576, [128, 1024], F32)]
            dma_s0 = {}
            for m0 in range(2):
                dma_s0[m0] = inc(dx.dma_start(
                    out=s0buf[m0][:], in_=s0sp[m0, :, h * HALF:(h + 1) * HALF]),
                    "s0a" if m0 % 2 == 0 else "s0b")
            st_e = {"sq_ss": [0, 0], "mom_pe": [0, 0], "cast_gg": [0, 0]}
            pe_after2 = {}
            mom_tail = [None]
            dwc8 = {}

            def issue_c8(m):
                if m >= 32:
                    return
                w(gg, "pe", pe_after2[m - 2] if m >= 2 else PAN["pe"])
                dk8 = "dwa" if m % 2 == 0 else "dwb"
                dwc8[m] = inc(gg.dma_start(out=wpan[m % 2][:],
                                           in_=down_p[m, :, 4096:8192]), dk8)

            issue_c8(0)
            issue_c8(1)
            for m in range(32):
                issue_c8(m + 1)
                dk = "dwa" if m % 2 == 0 else "dwb"
                w(pe, dk, dwc8[m])
                w(pe, "vv", hB_ready)
                if m >= 2:
                    w(pe, "vv", psfree[m - 2])
                ps = PB[:, 0:1024] if m % 2 == 0 else PB[:, 1024:2048]
                for tc in range(2):
                    pv = ps[:, tc * 512:(tc + 1) * 512]
                    for cc in range(32):
                        mm = pe.matmul(pv, wpan[m % 2][:, cc * 128:(cc + 1) * 128],
                                       rhs_h(cc, tc), start=(cc == 0), stop=(cc == 31))
                pe_after2[m] = inc(mm, "pe")
                if mom_tail[0] is not None:
                    mom_tail[0]()
                    mom_tail[0] = None
                s_f32 = s_f32b[m % 2]
                sqst = sqst2[m % 2]
                w(vv, "pe", pe_after2[m])
                w(vv, "s0a" if m % 2 == 0 else "s0b", dma_s0[m])
                w(vv, "pe", st_e["mom_pe"][m % 2])
                w(vv, "ss", st_e["cast_gg"][m % 2])
                w(vv, "ss", st_e["sq_ss"][m % 2])
                sv = inc(vv.tensor_add(s_f32[:], ps, s0buf[m % 2][:]), "vv")
                psfree[m] = sv
                if m + 2 < 32:
                    w(dx, "vv", sv)
                    dma_s0[m + 2] = inc(dx.dma_start(
                        out=s0buf[m % 2][:],
                        in_=s0sp[m + 2, :, h * HALF:(h + 1) * HALF]),
                        "s0a" if m % 2 == 0 else "s0b")
                w(ss, "vv", sv)
                w(ss, "pe", st_e["mom_pe"][m % 2])
                st_e["sq_ss"][m % 2] = inc(
                    ss.activation(sqst[:], s_f32[:], AF.Square), "ss")
                sqv = st_e["sq_ss"][m % 2]
                st_e["cast_gg"][m % 2] = inc(
                    ss.activation(sst[:, m * 1024:(m + 1) * 1024], s_f32[:],
                                  AF.Copy), "ss")

                def mk_mom(mv, s_f32v, sqstv, svv, sqv2):
                    def tailf():
                        w(pe, "vv", svv)
                        w(pe, "ss", sqv2)
                        for tc in range(2):
                            pe.matmul(PM[0:4, tc * 512:(tc + 1) * 512],
                                      statsW.ap()[:, mv * 4:(mv + 1) * 4],
                                      s_f32v[:, tc * 512:(tc + 1) * 512],
                                      start=(mv == 0), stop=(mv == 31))
                            mm2 = pe.matmul(PTf[0:4, tc * 512:(tc + 1) * 512],
                                            statsW.ap()[:, mv * 4:(mv + 1) * 4],
                                            sqstv[:, tc * 512:(tc + 1) * 512],
                                            start=(mv == 0), stop=(mv == 31))
                        st_e["mom_pe"][mv % 2] = inc(mm2, "pe")
                    return tailf

                mom_tail[0] = mk_mom(m, s_f32, sqst, sv, sqv)
            mom_tail[0]()
            s_ready_gg = cnt["ss"]
            mom_done = cnt["pe"]
            if dbg:
                w(dx, "ss", s_ready_gg)
                inc(dx.dma_start(out=d_sst[h], in_=sst[:]), "dbg")

            barrier(vv, pe, ss)
            m4sb = hv(H2 + 24576, [4, 1024], F32)
            m2sb = hv(H2 + 28672, [4, 1024], F32)
            momtm = hv(H2 + 12288, [128, 64], F32)
            w(vv, "pe", mom_done)
            vv.tensor_copy(m4sb[0:4, :], PM[0:4, 0:1024])
            mc = inc(vv.tensor_copy(m2sb[0:4, :], PTf[0:4, 0:1024]), "vv")
            w(pe, "vv", mc)
            for blk in range(8):
                pe.transpose(PB[:, blk * 4:(blk + 1) * 4],
                             m4sb[0:4, blk * 128:(blk + 1) * 128], p_idf32[0:4, 0:4])
                tp = pe.transpose(PB[:, 32 + blk * 4: 32 + (blk + 1) * 4],
                                  m2sb[0:4, blk * 128:(blk + 1) * 128],
                                  p_idf32[0:4, 0:4])
            inc(tp, "pe")
            w(vv, "pe")
            inc(vv.tensor_copy(momtm[:], PB[:, 0:64]), "vv")
            Sv1 = momtm[:, 0:32:4]
            Gv1 = momtm[:, 1:32:4]
            Gv2 = momtm[:, 2:32:4]
            GvB = momtm[:, 3:32:4]
            Sv2 = momtm[:, 32:64:4]
            Hv2 = momtm[:, 34:64:4]
            tmb = H2 + 13312
            tm = lambda i: hv(tmb + i * 64, [128, 8], F32)
            mu4, rs4, mu5, rs5, tA, tB, tC, Ar, Cr, Dr = (tm(i) for i in range(10))
            invD = 1.0 / D
            vsync(vv.tensor_scalar(mu4[:], Sv1, invD, None, op0=ALU.mult))
            vsync(vv.tensor_scalar(tA[:], Sv2, invD, None, op0=ALU.mult))
            vsync(vv.tensor_mul(tB[:], mu4[:], mu4[:]))
            v1 = vsync(vv.tensor_sub(tB[:], tB[:], tA[:]))
            w(ss, "vv", v1)
            s1 = inc(ss.activation(tB[:], tB[:], AF.Sqrt, bias=p_eps[:], scale=-1.0), "ss")
            w(vv, "ss", s1)
            vsync(vv.reciprocal(rs4[:], tB[:]))
            vsync(vv.tensor_scalar(tA[:], Gv1, invD, None, op0=ALU.mult))
            vsync(vv.tensor_scalar(tB[:], mu4[:], CGD, None, op0=ALU.mult))
            vsync(vv.tensor_sub(tA[:], tA[:], tB[:]))
            vsync(vv.tensor_mul(tA[:], tA[:], rs4[:]))
            vsync(vv.tensor_scalar(mu5[:], tA[:], CBD, None, op0=ALU.add))
            vsync(vv.tensor_scalar(tA[:], Gv2, 2.0 * invD, None, op0=ALU.mult))
            vsync(vv.tensor_mul(tA[:], tA[:], mu4[:]))
            vsync(vv.tensor_scalar(tB[:], Hv2, invD, None, op0=ALU.mult))
            vsync(vv.tensor_sub(tB[:], tB[:], tA[:]))
            vsync(vv.tensor_mul(tA[:], mu4[:], mu4[:]))
            vsync(vv.tensor_scalar(tA[:], tA[:], CG2D, None, op0=ALU.mult))
            vsync(vv.tensor_add(tB[:], tB[:], tA[:]))
            vsync(vv.tensor_mul(tB[:], tB[:], rs4[:]))
            vsync(vv.tensor_mul(tB[:], tB[:], rs4[:]))
            vsync(vv.tensor_scalar(tA[:], GvB, invD, None, op0=ALU.mult))
            vsync(vv.tensor_scalar(tC[:], mu4[:], CGBD, None, op0=ALU.mult))
            vsync(vv.tensor_sub(tA[:], tA[:], tC[:]))
            vsync(vv.tensor_mul(tA[:], tA[:], rs4[:]))
            vsync(vv.tensor_scalar(tA[:], tA[:], 2.0, None, op0=ALU.mult))
            vsync(vv.tensor_add(tB[:], tB[:], tA[:]))
            vsync(vv.tensor_scalar(tB[:], tB[:], CB2D, None, op0=ALU.add))
            vsync(vv.tensor_mul(tC[:], mu5[:], mu5[:]))
            v2 = vsync(vv.tensor_sub(tC[:], tC[:], tB[:]))
            w(ss, "vv", v2)
            s2 = inc(ss.activation(tC[:], tC[:], AF.Sqrt, bias=p_eps[:], scale=-1.0), "ss")
            w(vv, "ss", s2)
            vsync(vv.reciprocal(rs5[:], tC[:]))
            vsync(vv.tensor_mul(Ar[:], rs4[:], rs5[:]))
            vsync(vv.tensor_mul(Cr[:], mu4[:], Ar[:]))
            v3 = vsync(vv.tensor_mul(Dr[:], mu5[:], rs5[:]))
            rowsbf = hv(H2, [1, 4096], BF16)
            bcA = hv(H0, [128, 1024], F32)
            bcC = hv(H0 + 4096, [128, 1024], F32)
            bcB = hv(H0 + 8192, [128, 1024], F32)
            bcD = hv(H0 + 12288, [128, 1024], F32)
            w(pe, "vv", v3)
            rowcp = 0
            for rnd, (ta_, tb_) in enumerate([(Ar, Cr), (rs5, Dr)]):
                w(pe, "vv", rowcp)
                for blk in range(8):
                    pe.transpose(PB[0:1, blk * 128:(blk + 1) * 128],
                                 ta_[:, blk:blk + 1], p_idf32[:])
                    tp = pe.transpose(PB[0:1, 1024 + blk * 128: 1024 + (blk + 1) * 128],
                                      tb_[:, blk:blk + 1], p_idf32[:])
                inc(tp, "pe")
                w(vv, "pe")
                rowcp = inc(vv.tensor_copy(rowsbf[0:1, rnd * 2048:(rnd + 1) * 2048],
                                           PB[0:1, 0:2048]), "vv")
            w(pe, "vv")
            repcp = 0
            for i, bc in enumerate([bcA, bcC, bcB, bcD]):
                w(pe, "vv", repcp)
                pe.matmul(PM[:, 0:512], p_ones[:],
                          rowsbf[0:1, i * 1024: i * 1024 + 512], start=True, stop=True)
                mm = pe.matmul(PM[:, 512:1024], p_ones[:],
                               rowsbf[0:1, i * 1024 + 512:(i + 1) * 1024],
                               start=True, stop=True)
                inc(mm, "pe")
                w(vv, "pe")
                repcp = inc(vv.tensor_copy(bc[:], PM[:, 0:1024]), "vv")
            bc_ready = cnt["vv"]
            if dbg and h == 0:
                w(dx, "vv")
                for bi, bc in enumerate([bcA, bcC, bcB, bcD]):
                    inc(dx.dma_start(out=d_bc[bi], in_=bc[:]), "dbg")
                inc(dx.dma_start(out=d_mom, in_=momtm[:]), "dbg")
                inc(dx.dma_start(out=d_cc[:, 0:5], in_=constcol.ap()[:, 0:5]), "dbg")
                inc(dx.dma_start(out=d_tm[:, 0:8], in_=mu4[:]), "dbg")
                inc(dx.dma_start(out=d_tm[:, 8:16], in_=rs4[:]), "dbg")
                inc(dx.dma_start(out=d_tm[:, 16:24], in_=mu5[:]), "dbg")
                inc(dx.dma_start(out=d_tm[:, 24:32], in_=rs5[:]), "dbg")

            barrier(vv, ss, gg, pe, dx)
            natp = hv(H0 + 16384, [128, 8192], F32)
            ofm2 = [hv(H2 + 16384, [128, 1024], F32), hv(H2 + 20480, [128, 1024], F32)]
            s32b = [hv(H2 + 24576, [128, 1024], F32), hv(H2 + 28672, [128, 1024], F32)]
            t1b = [hv(H0 + 49152, [128, 1024], F32), hv(H0 + 53248, [128, 1024], F32)]
            t3b = [hv(H0 + 57344, [128, 1024], F32), hv(H0 + 61440, [128, 1024], F32)]
            ybl = [hv(H2 + 4096, [128, 1024], F32), hv(H2 + 8192, [128, 1024], F32)]
            obl = [hv(H2, [128, 1024], F32), hv(H3, [128, 1024], F32)]
            st10 = {"t1_vv": [0, 0], "s32_vv": [0, 0], "ofm_pe": [0, 0],
                    "natp_gg": 0, "tpc": [0, 0], "ost": {}, "gga": {}}
            for mg in range(4):
                w(vv, "gg", st10["natp_gg"])
                for mi in range(8):
                    m = mg * 8 + mi
                    par = mi % 2
                    s32 = s32b[par]
                    t1 = t1b[par]
                    t3 = t3b[par]
                    ofm = ofm2[par]
                    gsl = p_gfm.ap()[:, m:m + 1]
                    bsl = p_bfm.ap()[:, m:m + 1]
                    w(ss, "vv", st10["s32_vv"][par])
                    sc = inc(ss.activation(s32[:], sst[:, m * 1024:(m + 1) * 1024],
                                           AF.Copy), "ss")
                    w(vv, "ss", sc)
                    tv = inc(vv.scalar_tensor_tensor(t1[:], s32[:], 1.0, bcA[:],
                                                     op0=ALU.mult, op1=ALU.mult),
                             "vv")
                    st10["s32_vv"][par] = tv
                    w(gg, "vv", tv)
                    w(gg, "vv", st10["t1_vv"][par])
                    g2 = inc(gg.tensor_sub(t1[:], t1[:], bcC[:]), "gg")
                    vv.scalar_tensor_tensor(t3[:], bcB[:], bsl, bcD[:],
                                            op0=ALU.mult, op1=ALU.subtract)
                    w(vv, "gg", g2)
                    w(vv, "pe", st10["ofm_pe"][par])
                    ov = inc(vv.scalar_tensor_tensor(ofm[:], t1[:], gsl, t3[:],
                                                     op0=ALU.mult, op1=ALU.add), "vv")
                    st10["t1_vv"][par] = ov
                    w(pe, "vv", ov)
                    w(pe, "vv", st10["tpc"][par])
                    pbh = PB[:, 0:1024] if par == 0 else PB[:, 1024:2048]
                    for j in range(8):
                        tp = pe.transpose(pbh[:, j * 128:(j + 1) * 128],
                                          ofm[:, j * 128:(j + 1) * 128], p_idf32[:])
                    st10["ofm_pe"][par] = inc(tp, "pe")
                    w(vv, "pe", st10["ofm_pe"][par])
                    dstn = natp.rearrange("p (a r) -> p a r", a=8, r=1024)[
                        :, :, mi * 128:(mi + 1) * 128]
                    st10["tpc"][par] = inc(
                        vv.tensor_copy(dstn, pbh.rearrange("p (a r) -> p a r",
                                                           a=8, r=128)), "vv")
                natp_done = cnt["vv"]
                for t in range(NT):
                    row0 = h * HALF + t * 128
                    c0 = mg * 1024
                    yb = ybl[t % 2]
                    ob = obl[t % 2]
                    yk = "sy0" if t % 2 == 0 else "sy1"
                    sk = "st0" if t % 2 == 0 else "st1"
                    w(dx, "gg", st10["gga"].get(t - 2, 0))
                    yl = inc(dx.dma_start(out=yb[:],
                                          in_=y_nat[row0:row0 + 128, c0:c0 + 1024]),
                             yk)
                    w(gg, yk, yl)
                    w(gg, "vv", natp_done)
                    w(gg, sk, st10["ost"].get(t - 2, 0))
                    st10["gga"][t] = inc(gg.tensor_add(
                        ob[:], natp[:, t * 1024:(t + 1) * 1024], yb[:]), "gg")
                    w(dx, "gg", st10["gga"][t])
                    st10["ost"][t] = inc(dx.dma_start(
                        out=out[row0:row0 + 128, c0:c0 + 1024], in_=ob[:]), sk)
                st10["natp_gg"] = st10["gga"][NT - 1]
                st10["ost"] = {}
                st10["gga"] = {}

        phase_A(0)
        phase_B(0)
        phase_A(1)
        phase_B(1)
        phase_C(0)
        phase_C(1)

        for e in (pe, vv, ss, dx):
            w(e, "st0")
            w(e, "st1")
            e.nop().then_inc(sems["fin"], 1)
        w(gg, "st0")
        w(gg, "st1")
        gg.wait_ge(sems["fin"], 4)
        for k in CKEYS + DKEYS:
            gg.sem_clear(sems[k])

    return nc


def kernel(**inputs):
    x = np.ascontiguousarray(np.asarray(inputs["L0_attn_hidden_states"],
                                        dtype=np.float32))
    wq = np.asarray(inputs["L0_attn_wq"], dtype=np.float32)
    wk = np.asarray(inputs["L0_attn_wk"], dtype=np.float32)
    wv = np.asarray(inputs["L0_attn_wv"], dtype=np.float32)
    wo = np.asarray(inputs["L0_attn_wo"], dtype=np.float32)
    gate = np.asarray(inputs["L0_ffn_gate_w"], dtype=np.float32)
    up = np.asarray(inputs["L0_ffn_up_w"], dtype=np.float32)
    down = np.asarray(inputs["L0_ffn_down_w"], dtype=np.float32)
    fc1 = np.asarray(inputs["L0_ffn_fc1_w"], dtype=np.float32)
    fc2 = np.asarray(inputs["L0_ffn_fc2_w"], dtype=np.float32)
    ex = np.asarray(inputs["L0_ffn_expert_w"], dtype=np.float32)
    gam = np.asarray(inputs["L0_ffn_ln_gamma"], dtype=np.float32)
    bet = np.asarray(inputs["L0_ffn_ln_beta"], dtype=np.float32)

    shared = {
        "wq_p": _panels(np.ascontiguousarray(wq.T)),
        "wk_p": _panels(np.ascontiguousarray(wk.T)),
        "wv_p": _panels(np.ascontiguousarray(wv.T)),
        "wo_p": _panels(np.ascontiguousarray(wo.T)),
        "up_p": _panels(up),
        "down_p": _panels(down),
        "fc1_p": _panels(fc1),
        "fc2_p": _panels(fc2),
        "ex_p": _panels(ex),
        "gamma_fm": np.ascontiguousarray(gam.reshape(32, 128).T),
        "beta_fm": np.ascontiguousarray(bet.reshape(32, 128).T),
        "gate_p": np.ascontiguousarray(
            gate.reshape(32, 128, 8).transpose(1, 0, 2).reshape(128, 256)),
    }
    shared.update(_host_consts())

    nc = build_core_program()
    in_maps = []
    for b in range(NCORES):
        m = {"x_in": np.ascontiguousarray(x[b])}
        m.update(shared)
        in_maps.append(m)
    res = run_bass_kernel_spmd(nc, in_maps, core_ids=list(range(NCORES)))
    outs = [res.results[b]["out"] for b in range(NCORES)]
    return np.stack(outs, axis=0).astype(np.float32)


